# version 29
# baseline (speedup 1.0000x reference)
import numpy as np
import concourse.bass as bass
import concourse.mybir as mybir
from concourse.bass_utils import run_bass_kernel_spmd
from contextlib import ExitStack

F32 = mybir.dt.float32
BF16 = mybir.dt.bfloat16
AF = mybir.ActivationFunctionType
ALU = mybir.AluOpType

D = 1024
SEQ = 2048
NSEQ = 2
NH = 16
NCH = 8
QB = 512
NQB = SEQ // QB
EPS = 1e-5
ACT_SET = 6

ENGINES = ("t", "v", "s", "y", "g")
DMA_SEMS = ("g_w", "g_x", "g_x2a", "g_x2b", "g_qkst", "g_vst0", "g_vst1",
            "g_out0", "g_out1", "g_rc", "y_l0", "y_l1", "y_dn", "y_xp")
ALL_SEMS = ENGINES[:3] + DMA_SEMS


class Prog:
    def __init__(self):
        self.ops = {e: [] for e in ENGINES}
        self.cnt = {s: 0 for s in ALL_SEMS}
        self.waited = {e: {} for e in ENGINES}
        self.last = {}

    def emit(self, eng, fn, waits=(), inc=0, sem=None):
        if eng in ("v", "s") and fn is not None:
            inc = max(inc, 1)
            if self.cnt[eng]:
                waits = list(waits) + [("_self", self.cnt[eng])]
        ws = []
        for tok in waits:
            if tok is None:
                continue
            se, val = tok
            if se == "_self":
                ws.append((eng, val))
                continue
            if se == eng:
                continue
            if self.waited[eng].get(se, 0) >= val:
                continue
            self.waited[eng][se] = val
            ws.append((se, val))
        tok = None
        if inc:
            s = sem if sem is not None else eng
            self.cnt[s] += inc
            tok = (s, self.cnt[s])
            self.last[s] = tok
        self.ops[eng].append((ws, fn, inc, sem))
        return tok

    def barrier(self):
        for e in ENGINES:
            ws = []
            for s in ALL_SEMS:
                if s == e or self.cnt[s] == 0:
                    continue
                if self.waited[e].get(s, 0) >= self.cnt[s]:
                    continue
                self.waited[e][s] = self.cnt[s]
                ws.append((s, self.cnt[s]))
            self.ops[e].append((ws, None, 0, None))


class Pool:
    def __init__(self, handle, nbytes):
        self.h = handle
        self.nbytes = nbytes
        self.off = 0
        self.marks = {}

    def view(self, off, free_elems, dtype):
        per = free_elems * mybir.dt.size(dtype)
        assert off % 4 == 0 and off + per <= self.nbytes
        nf32 = (per + 3) // 4
        apf = self.h[:, off // 4: off // 4 + nf32]
        ap = apf if dtype == F32 else apf.bitcast(dtype)
        return ap[:, :free_elems]

    def alloc(self, free_elems, dtype, name=None):
        per = free_elems * mybir.dt.size(dtype)
        off = (self.off + 31) & ~31
        assert off + per <= self.nbytes, (off, per, self.nbytes)
        self.off = off + per
        if name:
            self.marks[name] = off
        return self.view(off, free_elems, dtype)


def r3(ap2d, blk=128):
    return ap2d.rearrange("p (b t) -> p b t", t=blk)


def build_nc():
    nc = bass.Bass("TRN2", target_bir_lowering=False)
    p = Prog()

    x_in = nc.dram_tensor("x", [NSEQ, SEQ, D], F32, kind="ExternalInput").ap()
    w_in = {n: nc.dram_tensor(n, [D, D], F32, kind="ExternalInput").ap()
            for n in ["zq", "zk", "wvT", "wrT", "wkT", "wfT", "zqT", "zkT"]}
    pp_in = {n: nc.dram_tensor(n, [128, NCH], F32, kind="ExternalInput").ap()
             for n in ["bq", "bk", "mur", "muk"]}
    bv_in = nc.dram_tensor("bv", [128, D], F32, kind="ExternalInput").ap()
    ones_in = nc.dram_tensor("ones", [128, 128], F32,
                             kind="ExternalInput").ap()
    out_d = nc.dram_tensor("out", [NSEQ, SEQ, D], F32,
                           kind="ExternalOutput").ap()
    s_qfm = nc.dram_tensor("s_qfm", [NCH, 128, SEQ], BF16,
                           kind="Internal").ap()
    s_kfm = nc.dram_tensor("s_kfm", [NCH, 128, SEQ], BF16,
                           kind="Internal").ap()
    s_vaug = nc.dram_tensor("s_vaug", [SEQ // 128, 128, NH * 65], BF16,
                            kind="Internal").ap()
    s_rc = nc.dram_tensor("s_rc", [8, 2 * QB], BF16, kind="Internal").ap()

    es = ExitStack()
    w_sb = es.enter_context(nc.sbuf_tensor("w_sb", [128, 3 * NCH * D], BF16))
    ones_bf = es.enter_context(nc.sbuf_tensor("ones_bf", [128, 128], BF16))
    bv_row = es.enter_context(nc.sbuf_tensor("bv_row", [128, D], BF16))
    smc = es.enter_context(nc.sbuf_tensor("smc", [128, 20 * NCH], F32))
    SM = {n: i for i, n in enumerate(
        ["bq", "bk", "mur", "muk", "murm1", "mukm1", "znq", "znk",
         "cqq", "sqq", "tzq", "cqk", "sqk", "tzk", "tmp0", "tmp1",
         "x1prev", "cone", "nsqq", "nsqk"])}
    lnst = es.enter_context(nc.sbuf_tensor("lnst", [128, 80], F32))
    LNI = {n: i for i, n in enumerate(
        ["sum", "ssq", "m", "var", "inv", "nminv", "minv", "t0"])}
    dump = es.enter_context(nc.sbuf_tensor("dump", [128, D], BF16))
    dump2 = es.enter_context(nc.sbuf_tensor("dump2", [128, D], BF16))

    def smcol(name, i=None):
        j = SM[name]
        a = smc[:, j * NCH:(j + 1) * NCH]
        return a if i is None else a[:, i:i + 1]

    def lncol(name, tt=None, alt=False):
        i = LNI[name] + (10 if alt else 0)
        a = lnst[:, i * 4:(i + 1) * 4]
        return a if tt is None else a[:, tt:tt + 1]

    POOL_BYTES = 140 * 1024
    pool_t = es.enter_context(
        nc.sbuf_tensor("pool", [128, POOL_BYTES // 4], F32))
    pool = Pool(pool_t, POOL_BYTES)

    x_sb = pool.alloc(4 * D, F32)
    Aq2 = [pool.alloc(NCH * QB, BF16),
           pool.alloc(NCH * QB, BF16)]
    Ak2 = [pool.alloc(NCH * QB, BF16),
           pool.alloc(NCH * QB, BF16)]
    Bh4 = [pool.alloc(NCH * QB // 2, BF16) for _ in range(4)]
    Cbh4 = [pool.alloc(NCH * QB // 2, BF16) for _ in range(4)]
    xn_tm2 = [pool.alloc(4 * D, BF16), pool.alloc(4 * D, BF16)]
    xn_fm2 = [pool.alloc(NCH * QB, BF16), pool.alloc(NCH * QB, BF16)]
    xn2 = dump2[:, :]
    lam_b2 = [pool.alloc(QB, BF16), pool.alloc(QB, BF16)]
    Dq = pool.alloc(QB, F32)
    qs_b = Dq
    Qqb = pool.alloc(NCH * QB, BF16)
    Kqb = pool.alloc(NCH * QB, BF16)
    v_st = pool.alloc(2 * NH * 65, BF16)
    prod_end = pool.off

    pool.off = 0
    x2_sb = pool.alloc(2 * D, F32)
    x1_sb = pool.alloc(4 * D, F32)
    Ksl = pool.alloc(2 * SEQ, BF16)
    Qsl = pool.alloc(2 * QB, BF16)
    vsl = pool.alloc(2 * (16 * 130 + 64), BF16)
    PT = pool.alloc(4 * 1024, BF16)
    attn_fm2 = [pool.alloc(NCH * QB, BF16),
                pool.alloc(NCH * QB, BF16)]
    attn_tm = pool.alloc(4 * D, BF16, "attn_tm")
    rcb = pool.alloc(2 * QB, BF16)
    den_sb = pool.alloc(QB, BF16)
    den_tm = pool.alloc(4 * NH, BF16)
    rc_tm = pool.alloc(4 * NH, F32)
    r_fm = pool.alloc(NCH * QB, BF16)
    of_fm = pool.alloc(NCH * QB, BF16)
    of_tm = pool.view(pool.marks["attn_tm"], 4 * D, BF16)
    sgE = pool.alloc(2 * QB, F32)
    k0b = pool.alloc(QB, BF16)
    x1n_fm = pool.alloc(NCH * QB, BF16, "x1n_fm")
    x1n_tm = pool.alloc(4 * D, BF16)
    xr_fm = pool.alloc(NCH * QB, BF16)
    xk_fm = pool.alloc(NCH * QB, BF16)
    out_stage = pool.alloc(2 * D, F32)
    attn_end = pool.off
    ka_fm = pool.view(pool.marks["x1n_fm"], NCH * QB, BF16)
    assert max(prod_end, attn_end) <= POOL_BYTES, (prod_end, attn_end)

    psc = es.enter_context(nc.psum_tensor("psc", [128, 3072], F32))
    p_attA = es.enter_context(nc.psum_tensor("p_attA", [128, 512], F32))
    p_attB = es.enter_context(nc.psum_tensor("p_attB", [128, 512], F32))
    p_poin = [psc[:, 0:512], psc[:, 512:1024], p_attA[:, :], p_attB[:, :]]
    p_v = [psc[:, 1024:1536], psc[:, 1536:2048]]
    p_lam = psc[:, 2048:2560]
    p_sw = psc[:, 2560:3072]
    p_ff = [psc[:, 2048:2560], psc[:, 2560:3072]]

    sems = {}

    def w_slot(wi, ci, lo, hi):
        o = (wi * NCH + ci) * D
        return w_sb[:, o + lo:o + hi]

    def fchunk(buf, ci):
        a = buf.rearrange("p (tt c i) -> p tt c i", tt=4, c=NCH)
        return a[:, :, ci:ci + 1, :].squeeze(2)

    def T(fn, waits=(), inc=0):
        return p.emit("t", fn, waits, inc)

    def V(fn, waits=(), inc=0):
        return p.emit("v", fn, waits, inc)

    def Sc(fn, waits=(), inc=0):
        return p.emit("s", fn, waits, inc)

    def Y(fn, waits=(), inc=0, sem=None):
        return p.emit("y", fn, waits, inc, sem)

    def G(fn, waits=(), inc=0, sem=None):
        return p.emit("g", fn, waits, inc, sem)

    Sc(lambda s: s.add_instruction(mybir.InstLoadActFuncSet(
        name=nc.get_next_instruction_name(), act_func_set_id=ACT_SET,
        ins=[], outs=[])), inc=1)
    G(lambda g: g.dma_start(out=ones_bf[:], in_=ones_in), inc=16, sem="g_w")
    G(lambda g: g.dma_start(out=bv_row[:], in_=bv_in), inc=16, sem="g_w")
    for n in ["bq", "bk", "mur", "muk"]:
        G(lambda g, n=n: g.dma_start(out=smcol(n), in_=pp_in[n]),
          inc=16, sem="g_w")
    tk_start = None
    for i, n in enumerate(["zqT", "zkT"]):
        tk_start = G(lambda g, n=n, i=i: g.dma_start(
            out=r3(w_sb[:, (i + 1) * NCH * D:(i + 2) * NCH * D], D),
            in_=w_in[n].rearrange("(c p) d -> p c d", p=128)),
            inc=16, sem="g_w")
    tk_ones = tk_bv = tk_start
    tk_pp = {n: tk_start for n in ["bq", "bk", "mur", "muk"]}
    tk_zt = {n: tk_start for n in ["zqT", "zkT"]}
    V(lambda v: v.tensor_copy(smcol("cone", 0), ones_bf[:, 0:1]),
      waits=[tk_start], inc=1)
    for n, wi, dst in [("zqT", 1, "znq"), ("zkT", 2, "znk")]:
        for ci in range(NCH):
            V(lambda v, wi=wi, ci=ci, dst=dst: v.scalar_tensor_tensor(
                dump[:], w_slot(wi, ci, 0, D), 1.0, w_slot(wi, ci, 0, D),
                ALU.mult, ALU.mult, accum_out=smcol(dst, ci)),
              waits=[tk_zt[n]], inc=1)
    for dst in ["znq", "znk"]:
        Sc(lambda s, dst=dst: s.activation(smcol(dst), smcol(dst), AF.Ln),
           waits=[p.last["v"]], inc=1)
        Sc(lambda s, dst=dst: s.activation(smcol(dst), smcol(dst), AF.Exp,
                                           scale=0.5), inc=1)
        V(lambda v, dst=dst: v.tensor_scalar(
            smcol(dst), smcol(dst), 1e-15, None, ALU.max),
          waits=[p.last["s"]], inc=1)
    for src, dst in [("mur", "murm1"), ("muk", "mukm1")]:
        V(lambda v, src=src, dst=dst: v.tensor_scalar(
            smcol(dst), smcol(src), -1.0, 1.0, ALU.mult, ALU.add),
          waits=[tk_pp[src]], inc=1)
    for b, zn, cq, sq, tz, nsq in [
            ("bq", "znq", "cqq", "sqq", "tzq", "nsqq"),
            ("bk", "znk", "cqk", "sqk", "tzk", "nsqk")]:
        Sc(lambda s, b=b: s.activation(smcol("tmp0"), smcol(b), AF.Exp,
                                       scale=2.0),
           waits=[tk_pp[b], p.last.get("v")], inc=1)
        Sc(lambda s, b=b: s.activation(smcol("tmp1"), smcol(b), AF.Exp,
                                       scale=-2.0), inc=1)
        tv = p.last["s"]
        V(lambda v, zn=zn, tz=tz: v.tensor_scalar(
            smcol(tz), smcol(zn), 2.0, None, ALU.mult), waits=[tv], inc=1)
        V(lambda v, sq=sq: v.tensor_sub(smcol(sq), smcol("tmp0"),
                                        smcol("tmp1")), inc=1)
        V(lambda v, sq=sq: v.tensor_scalar(smcol(sq), smcol(sq), 0.5, None,
                                           ALU.mult), inc=1)
        V(lambda v, sq=sq, nsq=nsq: v.tensor_scalar(
            smcol(nsq), smcol(sq), -1.0, None, ALU.mult), inc=1)
        V(lambda v: v.tensor_add(smcol("tmp0"), smcol("tmp0"),
                                 smcol("tmp1")), inc=1)
        V(lambda v: v.tensor_scalar(smcol("tmp0"), smcol("tmp0"), 0.5, None,
                                    ALU.mult), inc=1)
        V(lambda v, zn=zn: v.reciprocal(smcol("tmp1"), smcol(zn)), inc=1)
        V(lambda v, cq=cq: v.tensor_mul(smcol(cq), smcol("tmp0"),
                                        smcol("tmp1")), inc=1)

    def load_wset(names, wait_tok):
        tk = None
        for i, n in enumerate(names):
            tk = G(lambda g, n=n, i=i: g.dma_start(
                out=r3(w_sb[:, i * NCH * D:(i + 1) * NCH * D], D),
                in_=w_in[n].rearrange("(c p) d -> p c d", p=128)),
                waits=[wait_tok], inc=16, sem="g_w")
        return tk

    tk_wset = load_wset(["zq", "zk", "wvT"], p.last["v"])
    p.barrier()

    def ln_stats(src_tile, wtoks):
        for tt in range(4):
            Sc(lambda s, tt=tt: s.activation(
                dump[:], src_tile(tt), AF.Identity,
                accum_out=lncol("sum", tt)), waits=[wtoks[tt]], inc=1)
            V(lambda v, tt=tt: v.scalar_tensor_tensor(
                dump2[:], src_tile(tt), 1.0, src_tile(tt),
                ALU.mult, ALU.mult, accum_out=lncol("ssq", tt)),
              waits=[wtoks[tt]], inc=1)
        tS = p.last["s"]
        V(lambda v: v.tensor_scalar(lncol("m"), lncol("sum"), 1.0 / D, None,
                                    ALU.mult), waits=[tS], inc=1)
        V(lambda v: v.tensor_mul(lncol("t0"), lncol("m"), lncol("m")), inc=1)
        V(lambda v: v.tensor_scalar(lncol("t0"), lncol("t0"), EPS, None,
                                    ALU.subtract), inc=1)
        V(lambda v: v.scalar_tensor_tensor(
            lncol("var"), lncol("ssq"), 1.0 / D, lncol("t0"),
            ALU.mult, ALU.subtract), inc=1)
        tV = p.last["v"]
        Sc(lambda s: s.activation(lncol("inv"), lncol("var"), AF.Ln),
           waits=[tV], inc=1)
        Sc(lambda s: s.activation(lncol("inv"), lncol("inv"), AF.Exp,
                                  scale=-0.5), inc=1)
        tS2 = p.last["s"]
        V(lambda v: v.tensor_mul(lncol("minv"), lncol("m"), lncol("inv")),
          waits=[tS2], inc=1)
        V(lambda v: v.tensor_scalar(lncol("nminv"), lncol("minv"), -1.0, None,
                                    ALU.mult), inc=1)
        return p.last["v"]

    def ln_stats_v(src_tile, wtoks):
        for tt in range(4):
            V(lambda v, tt=tt: v.tensor_scalar(
                dump[:], src_tile(tt), 1.0, 0.0, ALU.mult, ALU.add,
                accum_out=lncol("sum", tt)), waits=[wtoks[tt]], inc=1)
            V(lambda v, tt=tt: v.scalar_tensor_tensor(
                dump2[:], src_tile(tt), 1.0, src_tile(tt),
                ALU.mult, ALU.mult, accum_out=lncol("ssq", tt)), inc=1)
        V(lambda v: v.tensor_scalar(lncol("m"), lncol("sum"), 1.0 / D, None,
                                    ALU.mult), inc=1)
        V(lambda v: v.tensor_mul(lncol("t0"), lncol("m"), lncol("m")), inc=1)
        V(lambda v: v.tensor_scalar(lncol("t0"), lncol("t0"), EPS, None,
                                    ALU.subtract), inc=1)
        tV = V(lambda v: v.scalar_tensor_tensor(
            lncol("var"), lncol("ssq"), 1.0 / D, lncol("t0"),
            ALU.mult, ALU.subtract), inc=1)
        Sc(lambda s: s.activation(lncol("inv"), lncol("var"), AF.Ln),
           waits=[tV], inc=1)
        tS2 = Sc(lambda s: s.activation(lncol("inv"), lncol("inv"), AF.Exp,
                                        scale=-0.5), inc=1)
        V(lambda v: v.tensor_mul(lncol("minv"), lncol("m"), lncol("inv")),
          waits=[tS2], inc=1)
        return V(lambda v: v.tensor_scalar(lncol("nminv"), lncol("minv"),
                                           -1.0, None, ALU.mult), inc=1)

    def lnv_red(src_tile, wtoks, alt=False):
        for tt in range(4):
            V(lambda v, tt=tt: v.tensor_scalar(
                dump[:], src_tile(tt), 1.0, 0.0, ALU.mult, ALU.add,
                accum_out=lncol("sum", tt, alt)), waits=[wtoks[tt]], inc=1)
            V(lambda v, tt=tt: v.scalar_tensor_tensor(
                dump2[:], src_tile(tt), 1.0, src_tile(tt),
                ALU.mult, ALU.mult, accum_out=lncol("ssq", tt, alt)),
              inc=1)
        V(lambda v: v.tensor_scalar(
            lncol("m", None, alt), lncol("sum", None, alt), 1.0 / D, None,
            ALU.mult), inc=1)
        V(lambda v: v.tensor_mul(lncol("t0", None, alt),
                                 lncol("m", None, alt),
                                 lncol("m", None, alt)), inc=1)
        V(lambda v: v.tensor_scalar(
            lncol("t0", None, alt), lncol("t0", None, alt), EPS, None,
            ALU.subtract), inc=1)
        return V(lambda v: v.scalar_tensor_tensor(
            lncol("var", None, alt), lncol("ssq", None, alt), 1.0 / D,
            lncol("t0", None, alt), ALU.mult, ALU.subtract), inc=1)

    def lnv_inv(tV, alt=False):
        Sc(lambda s: s.activation(lncol("inv", None, alt),
                                  lncol("var", None, alt), AF.Ln),
           waits=[tV], inc=1)
        return Sc(lambda s: s.activation(lncol("inv", None, alt),
                                         lncol("inv", None, alt), AF.Exp,
                                         scale=-0.5), inc=1)

    def lnv_fin(tS, alt=False):
        V(lambda v: v.tensor_mul(lncol("minv", None, alt),
                                 lncol("m", None, alt),
                                 lncol("inv", None, alt)),
          waits=[tS], inc=1)
        return V(lambda v: v.tensor_scalar(
            lncol("nminv", None, alt), lncol("minv", None, alt),
            -1.0, None, ALU.mult), inc=1)

    tk_vst_dma = [None, None]
    for s_i in range(NSEQ):
        pc = {"hev": [None] * 4, "lam_free": None, "x_cons": [],
              "lam_mm": [None, None],
              "fin": {(w, r): None for w in "qk" for r in (0, 1)},
              "bh_free": [None] * 4, "cb": [None] * 4,
              "head": None, "xnf_free": [None, None], "vev": [None, None]}

        def emit_head1(qb):
            par = qb % 2
            base = qb * QB
            xnf = xn_fm2[par]
            xntm = xn_tm2[par]
            tk_x = G(lambda g, s_i=s_i, base=base: g.dma_start(
                out=r3(x_sb, D),
                in_=x_in[s_i, base:base + QB, :].rearrange(
                    "(t p) d -> p t d", p=128)),
                waits=pc["x_cons"], inc=16, sem="g_x")
            tk_ln = ln_stats(lambda tt: x_sb[:, tt * D:(tt + 1) * D],
                             [tk_x] * 4)
            tk_xn = []
            for tt in range(4):
                tk_xn.append(V(lambda v, tt=tt, xntm=xntm: v.tensor_scalar(
                    xntm[:, tt * D:(tt + 1) * D],
                    x_sb[:, tt * D:(tt + 1) * D],
                    lncol("inv", tt), lncol("nminv", tt),
                    ALU.mult, ALU.add),
                    waits=[tk_ln, pc.get("xtm_free_%d" % par)], inc=1))
            tS_ln = p.last["s"]
            tk_xf = []
            for tt in range(4):
                tk_xf.append(Y(lambda y, tt=tt, xnf=xnf, xntm=xntm:
                               y.dma_start_transpose(
                    r3(xnf[:, tt * D:(tt + 1) * D]),
                    r3(xntm[:, tt * D:(tt + 1) * D])),
                    waits=[tk_xn[tt], pc["xnf_free"][par]], inc=16,
                    sem="y_xp"))
            tk_xf_all = tk_xf[-1]
            pc["xtm_free_%d" % par] = tk_xf_all
            return {"xf_all": tk_xf_all, "par": par, "base": base,
                    "xnf": xnf, "tk_xn3": tk_xn[3], "tS_ln": tS_ln}

        def emit_head2(hd):
            par = hd["par"]
            xnf = hd["xnf"]
            lamb = lam_b2[par]
            tk_xf_all = hd["xf_all"]
            tk_x2c = [None] * NCH
            tk_lmm = [None] * NCH
            for ci in range(NCH):
                tk_x2c[ci] = V(lambda v, ci=ci, xnf=xnf: v.tensor_mul(
                    r3(xn2[:, (ci % 2) * QB:(ci % 2 + 1) * QB]),
                    fchunk(xnf, ci), fchunk(xnf, ci)),
                    waits=[tk_xf_all,
                           tk_lmm[ci - 2] if ci >= 2 else
                           pc["lam_mm"][ci]], inc=1)
                tk_lmm[ci] = T(lambda t, ci=ci: t.matmul(
                    p_lam, ones_bf[:],
                    xn2[:, (ci % 2) * QB:(ci % 2 + 1) * QB],
                    start=(ci == 0), stop=(ci == NCH - 1),
                    skip_group_check=True),
                    waits=[tk_x2c[ci], tk_ones, pc["lam_free"]], inc=1)
            pc["lam_mm"] = [tk_lmm[6], tk_lmm[7]]
            tkv = V(lambda v, lamb=lamb: v.tensor_scalar(
                lamb[:], p_lam, 1.0, None, ALU.subtract),
                waits=[tk_lmm[NCH - 1]], inc=1)
            pc["lam_free"] = tkv
            Sc(lambda s, lamb=lamb: s.activation(lamb[:], lamb[:], AF.Ln),
               waits=[tkv], inc=1)
            tks = Sc(lambda s, lamb=lamb: s.activation(
                lamb[:], lamb[:], AF.Exp, scale=-1.0), inc=1)
            tk_lam = V(lambda v, lamb=lamb: v.tensor_scalar(
                lamb[:], lamb[:], -2.0, None, ALU.mult),
                waits=[tks], inc=1)
            pc["x_cons"] = [hd["tk_xn3"], hd["tS_ln"], tk_lmm[NCH - 1]]
            hd["lam"] = tk_lam
            return hd

        def emit_head(qb):
            return emit_head2(emit_head1(qb))

        def tails_mk(bnd):
            HB = NCH * QB // 2
            units = []
            for wi, (which, AA) in enumerate([("q", bnd["Aq"]),
                                              ("k", bnd["Ak"])]):
                for h in (0, 1):
                    units.append({
                        "u": wi * 2 + h, "which": which, "AA": AA,
                        "h": h, "Ah": AA[:, h * HB:(h + 1) * HB],
                        "tz": "tzq" if which == "q" else "tzk",
                        "tk": bnd["hev"][wi * NCH + h * 4 + 3],
                    })
            return {"bnd": bnd, "units": units}

        def tails_s1(st, idx=None):
            for un in (st["units"] if idx is None
                       else [st["units"][idx]]):
                un["tk"] = V(lambda v, un=un: v.tensor_mul(
                    Bh4[un["u"]][:], un["Ah"], un["Ah"]),
                    waits=[un["tk"], pc["bh_free"][un["u"]]], inc=1)

        def tails_s23(st, idx=None):
            for un in (st["units"] if idx is None
                       else [st["units"][idx]]):
                Sc(lambda s, un=un: s.activation(
                    Bh4[un["u"]][:], Bh4[un["u"]][:], AF.Ln,
                    bias=smcol("cone", 0)), waits=[un["tk"]], inc=1)
                un["tk"] = Sc(lambda s, un=un: s.activation(
                    Bh4[un["u"]][:], Bh4[un["u"]][:], AF.Exp, scale=0.5),
                    inc=1)

        def tails_s4(st, idx=None):
            for un in (st["units"] if idx is None
                       else [st["units"][idx]]):
                un["tk"] = V(lambda v, un=un: v.tensor_add(
                    un["Ah"], un["Ah"], Bh4[un["u"]][:]),
                    waits=[un["tk"]], inc=1)

        def tails_s5(st):
            for un in st["units"]:
                un["tk"] = Sc(lambda s, un=un: s.activation(
                    un["Ah"], un["Ah"], AF.Ln), waits=[un["tk"]], inc=1)

        def tails_s678(st):
            for un in st["units"]:
                for dc in range(un["h"] * 4, un["h"] * 4 + 4):
                    un["tk"] = V(lambda v, un=un, dc=dc: v.tensor_scalar(
                        un["AA"][:, dc * QB:(dc + 1) * QB],
                        un["AA"][:, dc * QB:(dc + 1) * QB],
                        smcol(un["tz"], dc), None, ALU.mult),
                        waits=[un["tk"]], inc=1)
                Sc(lambda s, un=un: s.activation(Bh4[un["u"]][:],
                                                 un["Ah"], AF.Exp),
                   waits=[un["tk"]], inc=1)
                un["tk"] = Sc(lambda s, un=un: s.activation(
                    un["Ah"], un["Ah"], AF.Exp, scale=-1.0), inc=1)
            for un in st["units"]:
                un["tk"] = V(lambda v, un=un: v.tensor_sub(
                    un["Ah"], Bh4[un["u"]][:], un["Ah"]),
                    waits=[un["tk"]], inc=1)
                pc["bh_free"][un["u"]] = un["tk"]

        def tails_fin(st):
            bnd = st["bnd"]
            base = bnd["base"]
            par = bnd["par"]
            units = st["units"]
            for wi, (which, outqb, store) in enumerate(
                    [("q", Qqb, s_qfm), ("k", Kqb, s_kfm)]):
                AA = bnd["Aq"] if which == "q" else bnd["Ak"]
                uns = units[wi * 2:wi * 2 + 2]
                tk_w2 = [None, None]
                for h in (0, 1):
                    un = uns[h]
                    tk_w2[h] = V(lambda v, un=un: v.tensor_mul(
                        Cbh4[un["u"]][:], un["Ah"], un["Ah"]),
                        waits=[un["tk"], pc["cb"][un["u"]]], inc=1)
                tmm = None
                for h in (0, 1):
                    for j in range(4):
                        dc = h * 4 + j
                        u = wi * 2 + h
                        tmm = T(lambda t, u=u, j=j, dc=dc: t.matmul(
                            p_sw, ones_bf[:],
                            Cbh4[u][:, j * QB:(j + 1) * QB],
                            start=(dc == 0), stop=(dc == NCH - 1),
                            skip_group_check=True),
                            waits=[tk_w2[h]], inc=(1 if j == 3 else 0))
                    pc["cb"][wi * 2 + h] = tmm
                V(lambda v: v.tensor_scalar(Dq[:], p_sw, 0.25, 1.0,
                                            ALU.mult, ALU.add),
                  waits=[tmm], inc=1)
                tV = p.last["v"]
                Sc(lambda s: s.activation(Dq[:], Dq[:], AF.Ln), waits=[tV],
                   inc=1)
                Sc(lambda s: s.activation(Dq[:], Dq[:], AF.Exp, scale=0.5),
                   inc=1)
                tS = p.last["s"]
                tkv = V(lambda v: v.tensor_scalar(Dq[:], Dq[:], 1.0, None,
                                                  ALU.add), waits=[tS],
                        inc=1)
                Sc(lambda s: s.activation(qs_b[:], Dq[:], AF.Ln),
                   waits=[tkv], inc=1)
                tks = Sc(lambda s: s.activation(qs_b[:], qs_b[:], AF.Exp,
                                                scale=-1.0), inc=1)
                qs_bf = Bh4[wi * 2][:, 0:QB]
                V(lambda v, qs_bf=qs_bf: v.tensor_scalar(
                    qs_bf, qs_b[:], 0.5, None, ALU.mult),
                  waits=[tks], inc=1)
                out4 = outqb.rearrange("p (tt c i) -> p c tt i",
                                       tt=4, c=NCH)
                a4 = AA.rearrange("p (c tt i) -> p c tt i", c=NCH, tt=4)
                q4 = qs_bf.rearrange("p (tt i) -> p tt i", tt=4) \
                    .unsqueeze(1).broadcast_to([128, NCH, 4, 128])
                tk_fin = V(lambda v, out4=out4, a4=a4, q4=q4:
                           v.tensor_tensor(out4, a4, q4, ALU.mult),
                           waits=[("g_qkst", p.cnt["g_qkst"])], inc=1)
                pc["fin"][(which, par)] = tk_fin
                pc["bh_free"][wi * 2] = tk_fin
                for tt in range(4):
                    G(lambda g, outqb=outqb, store=store, base=base, tt=tt:
                      g.dma_start(
                          out=store[:, :, base + tt * 128:
                                    base + (tt + 1) * 128].rearrange(
                              "c p i -> p c i"),
                          in_=r3(outqb[:, tt * D:(tt + 1) * D])),
                      waits=[tk_fin], inc=16, sem="g_qkst")

        def emit_tails(bnd):
            st = tails_mk(bnd)
            tails_s1(st)
            tails_s23(st)
            tails_s4(st)
            tails_s5(st)
            tails_s678(st)
            tails_fin(st)

        bnd_p = None
        hd = emit_head(0)
        for qb in range(NQB):
            base = hd["base"]
            par = hd["par"]
            xnf = hd["xnf"]
            lamb = lam_b2[par]
            tk_xf_all = hd["xf_all"]
            tk_lam = hd["lam"]
            hev_toks = {}
            hev_bank = {}

            def emit_proj(which, wslot, filler=None, par=par, xnf=xnf,
                          lamb=lamb, tk_xf_all=tk_xf_all, tk_lam=tk_lam):
                cq, sq = (("cqq", "sqq") if which == "q"
                          else ("cqk", "sqk"))
                for dc in range(NCH):
                    gidx = (0 if which == "q" else NCH) + dc
                    bank = gidx % 4
                    tk_pj = None
                    for ci in range(NCH):
                        tk_pj = T(lambda t, dc=dc, ci=ci, wslot=wslot,
                                  bank=bank, xnf=xnf: t.matmul(
                            p_poin[bank],
                            w_slot(wslot, ci, dc * 128, (dc + 1) * 128),
                            fchunk(xnf, ci),
                            start=(ci == 0), stop=(ci == NCH - 1),
                            skip_group_check=True),
                            waits=[tk_xf_all, tk_wset,
                                   hev_bank.get(gidx - 4, pc["hev"][gidx])
                                   if gidx < 4 else hev_bank.get(gidx - 4)],
                            inc=(1 if ci == NCH - 1 else 0))
                    AA = Aq2[par] if which == "q" else Ak2[par]
                    nsq = "nsqq" if which == "q" else "nsqk"
                    tk_h = Sc(lambda s, dc=dc, cq=cq, nsq=nsq, bank=bank,
                              AA=AA: s.activation(
                        AA[:, dc * QB:(dc + 1) * QB], p_poin[bank],
                        AF.Identity, scale=smcol(cq, dc),
                        bias=smcol(nsq, dc)),
                        waits=[tk_pj, pc["fin"][(which, par)]], inc=1)
                    hev_bank[gidx] = tk_h
                    V(lambda v, dc=dc, AA=AA, lamb=lamb: v.tensor_mul(
                        AA[:, dc * QB:(dc + 1) * QB],
                        AA[:, dc * QB:(dc + 1) * QB], lamb[:]),
                      waits=[tk_lam, tk_h], inc=1)
                    hev_toks[gidx] = V(lambda v, dc=dc, sq=sq, AA=AA:
                                       v.tensor_scalar(
                        AA[:, dc * QB:(dc + 1) * QB],
                        AA[:, dc * QB:(dc + 1) * QB],
                        smcol(sq, dc), None, ALU.add), inc=1)
                    if filler is not None:
                        filler(dc)

            def emit_vproj(tts, qb=qb, xnf=xnf, tk_xf_all=tk_xf_all):
                for tt in tts:
                    tk_vp = [None, None]
                    for hf in range(2):
                        for ci in range(NCH):
                            tk_vp[hf] = T(lambda t, tt=tt, hf=hf, ci=ci,
                                          xnf=xnf: t.matmul(
                                p_v[hf],
                                xnf[:, tt * D + ci * 128:
                                    tt * D + (ci + 1) * 128],
                                w_slot(2, ci, hf * 512, (hf + 1) * 512),
                                start=(ci == 0), stop=(ci == NCH - 1),
                                skip_group_check=True),
                                waits=[tk_xf_all, tk_vst_dma[tt % 2],
                                       pc["vev"][hf]],
                                inc=(1 if ci == NCH - 1 else 0))
                    vst = v_st[:, (tt % 2) * NH * 65:
                               (tt % 2 + 1) * NH * 65].rearrange(
                        "p (h x) -> p h x", x=65)
                    for hf in range(2):
                        pc["vev"][hf] = V(lambda v, hf=hf, vst=vst:
                                          v.tensor_tensor(
                            vst[:, hf * 8:(hf + 1) * 8, 0:64],
                            p_v[hf].rearrange("p (h x) -> p h x", x=64),
                            bv_row[:, hf * 512:(hf + 1) * 512].rearrange(
                                "p (h x) -> p h x", x=64), ALU.add),
                            waits=[tk_vp[hf], tk_bv], inc=1)
                    tk_pk = V(lambda v, vst=vst: v.tensor_copy(
                        vst[:, :, 64:65], ones_bf[:, 0:16].unsqueeze(2)),
                        waits=[tk_ones], inc=1)
                    tk_vst_dma[tt % 2] = G(
                        lambda g, tt=tt, qb=qb: g.dma_start(
                            out=s_vaug[qb * 4 + tt, :, :],
                            in_=v_st[:, (tt % 2) * NH * 65:
                                     (tt % 2 + 1) * NH * 65]),
                        waits=[tk_pk], inc=16, sem="g_vst%d" % (tt % 2))

            st = tails_mk(bnd_p) if bnd_p is not None else None

            def fil_q(dc, st=st):
                if st is not None and dc % 2 == 1:
                    tails_s1(st, dc // 2)

            def fil_k(dc, st=st):
                if st is None:
                    return
                i = dc // 2
                if dc % 2 == 1:
                    if i >= 1:
                        tails_s4(st, i - 1)
                    tails_s23(st, i)

            emit_proj("q", 0, fil_q)
            emit_proj("k", 1, fil_k)
            if st:
                tails_s4(st, 3)
                tails_s5(st)
                tails_s678(st)
            hd1 = emit_head1(qb + 1) if qb + 1 < NQB else None
            pc["hev"] = [hev_bank[12], hev_bank[13], hev_bank[14],
                         hev_bank[15]]
            emit_vproj((0, 1))
            if st:
                tails_fin(st)
            if hd1 is not None:
                hd = emit_head2(hd1)
            emit_vproj((2, 3))
            pc["xnf_free"][par] = p.last["t"]
            bnd_p = {"qb": qb, "base": base, "hev": hev_toks,
                     "Aq": Aq2[par], "Ak": Ak2[par], "par": par}
        tk_wset = load_wset(["wrT", "wkT", "wfT"], p.last["t"])
        emit_tails(bnd_p)
        bnd_p = None
        p.barrier()

        cy = {
            "qkl": [None, None], "pv": [None, None], "evac": None,
            "exp": {}, "pvg": {}, "rc": None, "dn": None, "ofx": None,
            "afm_free": [None, None], "x1stt": None, "x1ntm_free": None,
            "x2c": [None, None],
            "mix": None, "mix_r": None, "v2": None,
            "ffn_pu": [None, None],
            "of_free": None, "odma": [None, None], "rfree": None,
            "kafree": None, "xrkfree": None,
        }

        def emit_x2q(qb, tt, wait):
            return G(lambda g, tt=tt, base=qb * QB, s_i=s_i: g.dma_start(
                out=x2_sb[:, (tt % 2) * D:(tt % 2 + 1) * D],
                in_=x_in[s_i, base + tt * 128:base + (tt + 1) * 128, :]),
                waits=[wait], inc=16,
                sem=("g_x2a" if tt % 2 == 0 else "g_x2b"))

        def emit_bndA_y(bnd):
            tk_dn = Y(lambda y: y.dma_start(
                out=den_sb[0:16, :],
                in_=s_rc[:, :].rearrange("a (b j) -> (a b) j", b=2)),
                waits=[("g_rc", p.cnt["g_rc"]), cy.get("rcp")],
                inc=16, sem="y_dn")
            cy["dn"] = tk_dn
            tk_xa = []
            for tt in range(4):
                tk_xa.append(Y(lambda y, tt=tt, afm=attn_fm2[bnd["qb"] % 2]:
                               y.dma_start_transpose(
                    r3(attn_tm[:, tt * D:(tt + 1) * D]),
                    r3(afm[:, tt * D:(tt + 1) * D])),
                    waits=[bnd["evac7"], cy["x1stt"], cy["of_free"]],
                    inc=16, sem="y_xp"))
            cy["afm_free"][bnd["qb"] % 2] = tk_xa[-1]
            tk_dt = []
            for tt in range(4):
                tk_dt.append(Y(lambda y, tt=tt: y.dma_start_transpose(
                    den_tm[:, tt * 16:(tt + 1) * 16],
                    den_sb[0:16, tt * 128:(tt + 1) * 128]),
                    waits=[tk_dn], inc=16, sem="y_xp"))
            bnd["xa"] = tk_xa[-1]
            bnd["dt"] = tk_dt[-1]

        def emit_bndA_v(bnd):
            tk_rcp = V(lambda v: v.reciprocal(rc_tm[:], den_tm[:]),
                       waits=[bnd["dt"]], inc=1)
            cy["rcp"] = tk_rcp
            tk_div = []
            for tt in range(4):
                rc3 = rc_tm[:, tt * 16:(tt + 1) * 16].unsqueeze(2) \
                    .broadcast_to([128, 16, 64])
                av = attn_tm[:, tt * D:(tt + 1) * D].rearrange(
                    "p (h x) -> p h x", x=64)
                tk_div.append(V(lambda v, av=av, rc3=rc3: v.tensor_tensor(
                    av, av, rc3, ALU.mult),
                    waits=[bnd["xa"], tk_rcp], inc=1))
            bnd["tV2"] = lnv_red(
                lambda tt: attn_tm[:, tt * D:(tt + 1) * D],
                [tk_div[tt] for tt in range(4)])

        def emit_bndB(bnd):
            tS2 = lnv_inv(bnd["tV2"])
            tfin = lnv_fin(tS2)
            tk_x2t = bnd["x2t"]
            tk_x1 = []
            for tt in range(4):
                slo = (tt % 2) * D
                V(lambda v, tt=tt, slo=slo: v.tensor_scalar(
                    out_stage[:, slo:slo + D], x2_sb[:, slo:slo + D],
                    lncol("minv", tt), None, ALU.subtract),
                  waits=[tfin, tk_x2t[tt], cy["odma"][tt % 2]], inc=1)
                tk_x1.append(V(lambda v, tt=tt, slo=slo:
                               v.scalar_tensor_tensor(
                    x1_sb[:, tt * D:(tt + 1) * D],
                    attn_tm[:, tt * D:(tt + 1) * D],
                    lncol("inv", tt), out_stage[:, slo:slo + D],
                    ALU.mult, ALU.add), inc=1))
                if tt + 2 < 4:
                    tk_x2t[tt + 2] = emit_x2q(bnd["qb"], tt + 2, tk_x1[tt])
            cy["x2c"] = [tk_x1[2], tk_x1[3]]
            cy["x1stt"] = tk_x1[3]
            bnd["tV3"] = lnv_red(lambda tt: x1_sb[:, tt * D:(tt + 1) * D],
                                 tk_x1)

        def emit_bndC(bnd):
            tS3 = lnv_inv(bnd["tV3"])
            tfin3 = lnv_fin(tS3)
            tk_x1n = []
            for tt in range(4):
                tk_x1n.append(V(lambda v, tt=tt: v.tensor_scalar(
                    x1n_tm[:, tt * D:(tt + 1) * D],
                    x1_sb[:, tt * D:(tt + 1) * D],
                    lncol("inv", tt), lncol("nminv", tt),
                    ALU.mult, ALU.add),
                    waits=[tfin3, cy["x1ntm_free"]], inc=1))
            tk_xnf = []
            for tt in range(4):
                tk_xnf.append(Y(lambda y, tt=tt: y.dma_start_transpose(
                    r3(x1n_fm[:, tt * D:(tt + 1) * D]),
                    r3(x1n_tm[:, tt * D:(tt + 1) * D])),
                    waits=[tk_x1n[tt], cy["v2"]], inc=16, sem="y_xp"))
            cy["x1ntm_free"] = tk_xnf[-1]
            bnd["xnf_all"] = tk_xnf[-1]

        def emit_bndD(bnd):
            tk_xnf_all = bnd["xnf_all"]
            if bnd["qb"] == 0:
                V(lambda v: v.memzero(smcol("x1prev")), inc=1)
            x1n4 = x1n_fm.rearrange("p (tt c i) -> p tt c i", tt=4, c=NCH)
            dv8 = dump.bitcast(F32)[:, 0:8]
            dv24 = dump.bitcast(F32)[:, 8:32].rearrange(
                "p (a b) -> p a b", a=3)
            tk_mix = None
            for dst, mu, mum1, key in [(xr_fm, "mur", "murm1", "mix_r"),
                                       (xk_fm, "muk", "mukm1", "mix")]:
                d4 = dst.rearrange("p (tt c i) -> p tt c i", tt=4, c=NCH)
                for ci in range(NCH):
                    V(lambda v, d4=d4, mu=mu, ci=ci: v.tensor_scalar(
                        d4[:, :, ci:ci + 1, :].squeeze(2),
                        x1n4[:, :, ci:ci + 1, :].squeeze(2),
                        smcol(mu, ci), None, ALU.mult),
                      waits=[tk_xnf_all, cy["xrkfree"]], inc=1)
                    V(lambda v, d4=d4, mum1=mum1, ci=ci:
                      v.scalar_tensor_tensor(
                        d4[:, :, ci:ci + 1, 1:128].squeeze(2),
                        x1n4[:, :, ci:ci + 1, 0:127].squeeze(2),
                        smcol(mum1, ci),
                        d4[:, :, ci:ci + 1, 1:128].squeeze(2),
                        ALU.mult, ALU.add), inc=1)
                b0 = d4[:, 0:1, :, 0:1].squeeze(3).squeeze(1)
                V(lambda v, mum1=mum1: v.tensor_mul(
                    dv8, smcol("x1prev"), smcol(mum1)), inc=1)
                V(lambda v, b0=b0: v.tensor_add(b0, dv8, b0), inc=1)
                src3 = x1n4[:, 0:3, :, 127:128].squeeze(3)
                mb3 = smcol(mum1).unsqueeze(1).broadcast_to([128, 3, NCH])
                V(lambda v, src3=src3, mb3=mb3: v.tensor_mul(
                    dv24, src3, mb3), inc=1)
                b3 = d4[:, 1:4, :, 0:1].squeeze(3)
                tk_mix = V(lambda v, b3=b3: v.tensor_add(b3, dv24, b3),
                           inc=1)
                cy[key] = tk_mix
            prevsrc = x1n4[:, 3:4, :, 127:128].squeeze(3).squeeze(1)
            tk_mix = V(lambda v: v.tensor_copy(smcol("x1prev"), prevsrc),
                       waits=[tk_mix], inc=1)
            cy["mix"] = tk_mix
            cy["kafree"] = tk_xnf_all

        def emit_ffn1R(bnd):
            tk_sg = bnd["sg"] = [None] * NCH
            pu = cy["ffn_pu"]
            for dc in range(NCH):
                tk_rp = None
                for ci in range(NCH):
                    tk_rp = T(lambda t, dc=dc, ci=ci: t.matmul(
                        p_ff[dc % 2],
                        w_slot(0, ci, dc * 128, (dc + 1) * 128),
                        fchunk(xr_fm, ci),
                        start=(ci == 0), stop=(ci == NCH - 1),
                        skip_group_check=True),
                        waits=[cy["mix_r"], tk_wset, pu[dc % 2]],
                        inc=(1 if ci == NCH - 1 else 0))
                if dc % 2 == 0:
                    continue
                tk_e = Sc(lambda s: s.activation(
                    sgE[:], psc[:, 2048:3072], AF.Exp, scale=-1.0),
                    waits=[tk_rp, tk_sg[dc - 2] if dc >= 2
                           else cy.get("sgfree")], inc=1)
                Sc(lambda s: s.activation(sgE[:], sgE[:], AF.Ln,
                                          bias=smcol("cone", 0)), inc=1)
                tk_sg[dc] = tk_sg[dc - 1] = Sc(
                    lambda s, dc=dc: s.activation(
                        r_fm[:, (dc - 1) * QB:(dc + 1) * QB], sgE[:],
                        AF.Exp, scale=-1.0),
                    waits=[cy["rfree"]], inc=1)
                pu[0] = pu[1] = tk_e

        def emit_ffn1K(bnd):
            tk_ka = bnd["ka"] = [None] * NCH
            pu = cy["ffn_pu"]
            for dc in range(NCH):
                tk_kp = None
                for ci in range(NCH):
                    tk_kp = T(lambda t, dc=dc, ci=ci: t.matmul(
                        p_ff[dc % 2],
                        w_slot(1, ci, dc * 128, (dc + 1) * 128),
                        fchunk(xk_fm, ci),
                        start=(ci == 0), stop=(ci == NCH - 1),
                        skip_group_check=True),
                        waits=[cy["mix"], pu[dc % 2]],
                        inc=(1 if ci == NCH - 1 else 0))
                V(lambda v, dc=dc: v.tensor_scalar(
                    k0b[:], p_ff[dc % 2], 0.0, None, ALU.max),
                  waits=[tk_kp, cy["kafree"]], inc=1)
                tk_ka[dc] = V(lambda v, dc=dc: v.tensor_mul(
                    fchunk(ka_fm, dc), r3(k0b[:]), r3(k0b[:])), inc=1)
                pu[dc % 2] = tk_ka[dc]
                if dc == NCH - 1:
                    cy["xrkfree"] = tk_kp

        def emit_ffn1V(bnd):
            tk_sg = bnd["sg"]
            tk_ka = bnd["ka"]
            pu = cy["ffn_pu"]
            tk_of = [None] * NCH
            tk_v2 = None
            for dc in range(NCH):
                for ci in range(NCH):
                    tk_v2 = T(lambda t, dc=dc, ci=ci: t.matmul(
                        p_ff[dc % 2],
                        w_slot(2, ci, dc * 128, (dc + 1) * 128),
                        fchunk(ka_fm, ci),
                        start=(ci == 0), stop=(ci == NCH - 1),
                        skip_group_check=True),
                        waits=[tk_ka[NCH - 1], pu[dc % 2]],
                        inc=(1 if ci == NCH - 1 else 0))
                tk_of[dc] = V(lambda v, dc=dc: v.tensor_tensor(
                    fchunk(of_fm, dc),
                    r3(r_fm[:, dc * QB:(dc + 1) * QB]),
                    r3(p_ff[dc % 2]), ALU.mult),
                    waits=[tk_v2, tk_sg[dc],
                           cy["ofx"] if dc == 0 else None], inc=1)
                pu[dc % 2] = tk_of[dc]
            cy["v2"] = tk_v2
            cy["rfree"] = tk_of[NCH - 1]
            cy["sgfree"] = tk_sg[NCH - 1]
            bnd["of_last"] = tk_of[-1]

        def emit_ffn2(bnd):
            tk_ot = []
            for tt in range(4):
                tk_ot.append(Y(lambda y, tt=tt: y.dma_start_transpose(
                    r3(of_tm[:, tt * D:(tt + 1) * D]),
                    r3(of_fm[:, tt * D:(tt + 1) * D])),
                    waits=[bnd["of_last"], cy["of_free"], cy["x1stt"]],
                    inc=16, sem="y_xp"))
            cy["ofx"] = tk_ot[-1]
            bnd["tV4"] = lnv_red(
                lambda tt: of_tm[:, tt * D:(tt + 1) * D],
                [tk_ot[-1]] * 4, alt=True)

        def emit_ffn3(bnd):
            tS4 = lnv_inv(bnd["tV4"], alt=True)
            tfin4 = lnv_fin(tS4, alt=True)
            jbase = bnd["base"]
            tk_o = None
            for tt in range(4):
                slo = (tt % 2) * D
                V(lambda v, tt=tt, slo=slo: v.tensor_scalar(
                    out_stage[:, slo:slo + D],
                    x1_sb[:, tt * D:(tt + 1) * D],
                    lncol("minv", tt, True), None, ALU.subtract),
                  waits=[tfin4, cy["odma"][tt % 2]], inc=1)
                tk_o = V(lambda v, tt=tt, slo=slo: v.scalar_tensor_tensor(
                    out_stage[:, slo:slo + D],
                    of_tm[:, tt * D:(tt + 1) * D],
                    lncol("inv", tt, True), out_stage[:, slo:slo + D],
                    ALU.mult, ALU.add), inc=1)
                cy["odma"][tt % 2] = G(lambda g, jbase=jbase, tt=tt,
                                       slo=slo, s_i=s_i: g.dma_start(
                    out=out_d[s_i, jbase + tt * 128:
                              jbase + (tt + 1) * 128, :],
                    in_=out_stage[:, slo:slo + D]), waits=[tk_o], inc=16,
                    sem=("g_out0" if tt % 2 == 0 else "g_out1"))
            cy["of_free"] = tk_o

        pend_b = None
        for qb in range(NQB):
            base = qb * QB
            bnd_cur = {"qb": qb, "base": base, "x2t": [None] * 4}
            tk_qkl = [None] * 8
            tk_pv = [None] * 8
            tk_evac = [None] * 8
            tk_pvg = {}
            tk_exp = {}

            def emit_loads(pr, base=base):
                sl = pr % 2
                sname = "y_l%d" % sl
                wk = [tk_qkl[pr - 2] if pr >= 2 else cy["qkl"][sl]]
                wv_ = [tk_pv[pr - 2] if pr >= 2 else cy["pv"][sl]]
                Y(lambda y, pr=pr, sl=sl: y.dma_start(
                    out=Ksl[:, sl * SEQ:(sl + 1) * SEQ],
                    in_=s_kfm[pr, :, :]), waits=wk, inc=16, sem=sname)
                Y(lambda y, pr=pr, sl=sl, base=base: y.dma_start(
                    out=Qsl[:, sl * QB:(sl + 1) * QB],
                    in_=s_qfm[pr, :, base:base + QB]), waits=wk,
                  inc=16, sem=sname)
                tk = Y(lambda y, pr=pr, sl=sl: y.dma_start(
                    out=vsl[:, sl * 2144:sl * 2144 + 2080].rearrange(
                        "p (k x) -> p k x", k=16),
                    in_=s_vaug[:, :, 130 * pr:130 * (pr + 1)].rearrange(
                        "k p x -> p k x")), waits=wv_, inc=16, sem=sname)
                return tk

            tk_ld = [None] * 8
            if cy.get("preld") is not None:
                tk_ld[0], tk_ld[1] = cy["preld"]
                cy["preld"] = None
            else:
                tk_ld[0] = emit_loads(0)
                tk_ld[1] = emit_loads(1)

            def emit_pv(pr, c):
                sl = pr % 2
                po = (c % 4) * 1024
                tk = None
                vof = sl * 2144 + c * 130
                T(lambda t, c=c, po=po, vof=vof: t.matmul(
                    p_attA[:, :], vsl[:, vof:vof + 128],
                    PT[:, po:po + 512],
                    start=(c == 0), stop=(c == 15),
                    skip_group_check=True),
                    waits=[tk_exp[(pr, c)], tk_ld[pr],
                           (tk_evac[pr - 1] if pr >= 1 else cy["evac"])
                           if c == 0 else None], inc=0)
                tk = T(lambda t, c=c, po=po, vof=vof: t.matmul(
                    p_attB[:, :], vsl[:, vof + 65:vof + 193],
                    PT[:, po + 512:po + 1024],
                    start=(c == 0), stop=(c == 15),
                    skip_group_check=True), inc=1)
                tk_pvg[(pr, c)] = tk
                if c == 15:
                    tk_pv[pr] = tk

            for pr in range(8):
                if pr == 0 and pend_b is not None:
                    emit_bndA_y(pend_b)
                sl = pr % 2
                for c in range(16):
                    reg = (c % 2) * 1024
                    kof = sl * SEQ + c * 128
                    qof = sl * QB
                    if c >= 2:
                        wre = tk_exp.get((pr, c - 2))
                    elif pr >= 1:
                        wre = tk_exp[(pr - 1, 14 + c)]
                    else:
                        wre = cy["exp"].get(14 + c)
                    T(lambda t, kof=kof, qof=qof, reg=reg: t.matmul(
                        psc[:, reg:reg + 512],
                        Ksl[0:64, kof:kof + 128],
                        Qsl[0:64, qof:qof + QB],
                        start=True, stop=True, skip_group_check=True),
                        waits=[tk_ld[pr], wre], inc=0)
                    tk_qk = T(lambda t, kof=kof, qof=qof, reg=reg:
                              t.matmul(
                        psc[:, reg + 512:reg + 1024],
                        Ksl[64:128, kof:kof + 128],
                        Qsl[64:128, qof:qof + QB],
                        start=True, stop=True, skip_group_check=True),
                        inc=1)
                    if c == 15:
                        tk_qkl[pr] = tk_qk
                    if c >= 1:
                        emit_pv(pr, c - 1)
                    if c >= 4:
                        wslot = tk_pvg.get((pr, c - 4))
                    elif pr >= 1:
                        wslot = tk_pvg.get((pr - 1, 12 + c))
                    else:
                        wslot = cy["pvg"].get(12 + c)
                    po = (c % 4) * 1024
                    tk_exp[(pr, c)] = Sc(
                        lambda s, reg=reg, po=po: s.activation(
                            PT[:, po:po + 1024], psc[:, reg:reg + 1024],
                            AF.Exp, scale=0.125),
                        waits=[tk_qk, wslot], inc=1)
                emit_pv(pr, 15)
                V(lambda v: v.tensor_copy(rcb[0:1, 0:QB],
                                          p_attA[64:65, :]),
                  waits=[tk_pv[pr], cy.get("rcst")], inc=1)
                tk_rcb = V(lambda v: v.tensor_copy(rcb[0:1, QB:2 * QB],
                                                   p_attB[64:65, :]),
                           inc=1)
                cy["rcst"] = G(lambda g, pr=pr: g.dma_start(
                    out=s_rc[pr, :].unsqueeze(0), in_=rcb[0:1, :]),
                  waits=[tk_rcb, cy["dn"] if pr == 0 else None],
                  inc=16, sem="g_rc")
                V(lambda v, pr=pr, afm=attn_fm2[qb % 2]: v.tensor_copy(
                    fchunk(afm, pr)[0:64, :, :],
                    r3(p_attA[0:64, :])),
                  waits=[cy["afm_free"][qb % 2]], inc=1)
                tk_evac[pr] = V(lambda v, pr=pr, afm=attn_fm2[qb % 2]:
                                v.tensor_copy(
                    fchunk(afm, pr)[64:128, :, :],
                    r3(p_attB[0:64, :])), inc=1)
                if pr + 2 < 8:
                    tk_ld[pr + 2] = emit_loads(pr + 2)
                if pr == 2:
                    bnd_cur["x2t"][0] = emit_x2q(qb, 0, cy["x2c"][0])
                    bnd_cur["x2t"][1] = emit_x2q(qb, 1, cy["x2c"][1])
                if pend_b is not None:
                    if pr == 0:
                        emit_bndA_v(pend_b)
                    elif pr == 1:
                        emit_bndB(pend_b)
                    elif pr == 2:
                        emit_bndC(pend_b)
                    elif pr == 3:
                        emit_bndD(pend_b)
                    elif pr == 4:
                        emit_ffn1R(pend_b)
                    elif pr == 5:
                        emit_ffn1K(pend_b)
                    elif pr == 6:
                        emit_ffn1V(pend_b)
                        emit_ffn2(pend_b)
                    elif pr == 7:
                        emit_ffn3(pend_b)
                if pr == 7 and qb + 1 < NQB:
                    nbase = (qb + 1) * QB
                    cy["qkl"] = [tk_qkl[6], tk_qkl[7]]
                    cy["pv"] = [tk_pv[6], tk_pv[7]]
                    cy["preld"] = [emit_loads(0, base=nbase),
                                   emit_loads(1, base=nbase)]

            cy["qkl"] = [tk_qkl[6], tk_qkl[7]]
            cy["pv"] = [tk_pv[6], tk_pv[7]]
            cy["evac"] = tk_evac[7]
            cy["exp"] = {14: tk_exp[(7, 14)], 15: tk_exp[(7, 15)]}
            cy["pvg"] = {12 + i: tk_pvg[(7, 12 + i)] for i in range(4)}
            bnd_cur["evac7"] = tk_evac[7]
            pend_b = bnd_cur

        emit_bndA_y(pend_b)
        emit_bndA_v(pend_b)
        emit_bndB(pend_b)
        emit_bndC(pend_b)
        emit_bndD(pend_b)
        emit_ffn1R(pend_b)
        emit_ffn1K(pend_b)
        emit_ffn1V(pend_b)
        if s_i + 1 < NSEQ:
            tk_w01 = load_wset(["zq", "zk"], p.last["t"])
        emit_ffn2(pend_b)
        emit_ffn3(pend_b)
        if s_i + 1 < NSEQ:
            tk_wset = G(lambda g: g.dma_start(
                out=r3(w_sb[:, 2 * NCH * D:3 * NCH * D], D),
                in_=w_in["wvT"].rearrange("(c p) d -> p c d", p=128)),
                waits=[tk_w01], inc=16, sem="g_w")
        pend_b = None
        p.barrier()

    for s in ["g_w", "g_x", "g_x2a", "g_x2b", "g_qkst", "g_vst0",
              "g_vst1", "g_out0", "g_out1"]:
        if p.cnt[s]:
            G(lambda g, s=s: g.wait_ge(sems[s], p.cnt[s]))

    for s in ALL_SEMS:
        sems[s] = es.enter_context(nc.semaphore("sem_" + s))
    with nc.Block() as block, \
            nc.allow_low_precision(
                reason="bf16 intermediates; validated vs tolerance"):

        def replay(name, engobj):
            for ws, fn, inc, sem in p.ops[name]:
                for se, val in ws:
                    engobj.wait_ge(sems[se], val)
                if fn is None:
                    continue
                ins = fn(engobj)
                if inc:
                    ins.then_inc(sems[sem if sem else name], inc)

        @block.tensor
        def _(t):
            replay("t", t)

        @block.vector
        def _(v):
            replay("v", v)

        @block.scalar
        def _(s):
            replay("s", s)

        @block.sync
        def _(y):
            replay("y", y)

        @block.gpsimd
        def _(g):
            replay("g", g)

    es.close()
    return nc


def kernel(**inputs):
    x = np.ascontiguousarray(inputs["L0_attn_x"], dtype=np.float32)
    f32 = np.float32

    def pp(v):
        return np.ascontiguousarray(
            np.asarray(v, f32).reshape(NCH, 128).T)

    base = {
        "zq": np.ascontiguousarray(np.asarray(inputs["L0_attn_z_q"], f32)),
        "zk": np.ascontiguousarray(np.asarray(inputs["L0_attn_z_k"], f32)),
        "wvT": np.ascontiguousarray(np.asarray(inputs["L0_attn_w_v"], f32).T),
        "wrT": np.ascontiguousarray(np.asarray(inputs["L0_ffn_W_r"], f32).T),
        "wkT": np.ascontiguousarray(np.asarray(inputs["L0_ffn_W_k"], f32).T),
        "wfT": np.ascontiguousarray(np.asarray(inputs["L0_ffn_W_v"], f32).T),
        "zqT": np.ascontiguousarray(np.asarray(inputs["L0_attn_z_q"], f32).T),
        "zkT": np.ascontiguousarray(np.asarray(inputs["L0_attn_z_k"], f32).T),
        "bq": pp(inputs["L0_attn_b_q"]),
        "bk": pp(inputs["L0_attn_b_k"]),
        "mur": pp(inputs["L0_ffn_mu_r"]),
        "muk": pp(inputs["L0_ffn_mu_k"]),
        "bv": np.ascontiguousarray(np.broadcast_to(
            np.asarray(inputs["L0_attn_b_v"], f32), (128, D))),
        "ones": np.ones((128, 128), f32),
    }
    nc = build_nc()
    in_maps = []
    for core in range(8):
        m = dict(base)
        m["x"] = np.ascontiguousarray(x[2 * core:2 * core + 2])
        in_maps.append(m)
    res = run_bass_kernel_spmd(nc, in_maps, core_ids=list(range(8)))
    out = np.concatenate([r["out"] for r in res.results], axis=0)
    return np.ascontiguousarray(out.reshape(16, SEQ, D).astype(np.float32))


# version 30
# speedup vs baseline: 1.0113x; 1.0113x over previous
import numpy as np
import concourse.bass as bass
import concourse.mybir as mybir
from concourse.bass_utils import run_bass_kernel_spmd
from contextlib import ExitStack

F32 = mybir.dt.float32
BF16 = mybir.dt.bfloat16
AF = mybir.ActivationFunctionType
ALU = mybir.AluOpType

D = 1024
SEQ = 2048
NSEQ = 2
NH = 16
NCH = 8
QB = 512
NQB = SEQ // QB
EPS = 1e-5
ACT_SET = 6

ENGINES = ("t", "v", "s", "y", "g")
DMA_SEMS = ("g_w", "g_x", "g_x2a", "g_x2b", "g_qkst", "g_vst0", "g_vst1",
            "g_out0", "g_out1", "g_rc", "y_l0", "y_l1", "y_dn", "y_xp")
ALL_SEMS = ENGINES[:3] + DMA_SEMS


class Prog:
    def __init__(self):
        self.ops = {e: [] for e in ENGINES}
        self.cnt = {s: 0 for s in ALL_SEMS}
        self.waited = {e: {} for e in ENGINES}
        self.last = {}

    def emit(self, eng, fn, waits=(), inc=0, sem=None):
        if eng in ("v", "s") and fn is not None:
            inc = max(inc, 1)
            if self.cnt[eng]:
                waits = list(waits) + [("_self", self.cnt[eng])]
        ws = []
        for tok in waits:
            if tok is None:
                continue
            se, val = tok
            if se == "_self":
                ws.append((eng, val))
                continue
            if se == eng:
                continue
            if self.waited[eng].get(se, 0) >= val:
                continue
            self.waited[eng][se] = val
            ws.append((se, val))
        tok = None
        if inc:
            s = sem if sem is not None else eng
            self.cnt[s] += inc
            tok = (s, self.cnt[s])
            self.last[s] = tok
        self.ops[eng].append((ws, fn, inc, sem))
        return tok

    def barrier(self):
        for e in ENGINES:
            ws = []
            for s in ALL_SEMS:
                if s == e or self.cnt[s] == 0:
                    continue
                if self.waited[e].get(s, 0) >= self.cnt[s]:
                    continue
                self.waited[e][s] = self.cnt[s]
                ws.append((s, self.cnt[s]))
            self.ops[e].append((ws, None, 0, None))


class Pool:
    def __init__(self, handle, nbytes):
        self.h = handle
        self.nbytes = nbytes
        self.off = 0
        self.marks = {}

    def view(self, off, free_elems, dtype):
        per = free_elems * mybir.dt.size(dtype)
        assert off % 4 == 0 and off + per <= self.nbytes
        nf32 = (per + 3) // 4
        apf = self.h[:, off // 4: off // 4 + nf32]
        ap = apf if dtype == F32 else apf.bitcast(dtype)
        return ap[:, :free_elems]

    def alloc(self, free_elems, dtype, name=None):
        per = free_elems * mybir.dt.size(dtype)
        off = (self.off + 31) & ~31
        assert off + per <= self.nbytes, (off, per, self.nbytes)
        self.off = off + per
        if name:
            self.marks[name] = off
        return self.view(off, free_elems, dtype)


def r3(ap2d, blk=128):
    return ap2d.rearrange("p (b t) -> p b t", t=blk)


def build_nc():
    nc = bass.Bass("TRN2", target_bir_lowering=False)
    p = Prog()

    x_in = nc.dram_tensor("x", [NSEQ, SEQ, D], F32, kind="ExternalInput").ap()
    w_in = {n: nc.dram_tensor(n, [D, D], F32, kind="ExternalInput").ap()
            for n in ["zq", "zk", "wvT", "wrT", "wkT", "wfT", "zqT", "zkT"]}
    pp_in = {n: nc.dram_tensor(n, [128, NCH], F32, kind="ExternalInput").ap()
             for n in ["bq", "bk", "mur", "muk"]}
    bv_in = nc.dram_tensor("bv", [128, D], F32, kind="ExternalInput").ap()
    ones_in = nc.dram_tensor("ones", [128, 128], F32,
                             kind="ExternalInput").ap()
    out_d = nc.dram_tensor("out", [NSEQ, SEQ, D], F32,
                           kind="ExternalOutput").ap()
    s_qfm = nc.dram_tensor("s_qfm", [NCH, 128, SEQ], BF16,
                           kind="Internal").ap()
    s_kfm = nc.dram_tensor("s_kfm", [NCH, 128, SEQ], BF16,
                           kind="Internal").ap()
    s_vaug = nc.dram_tensor("s_vaug", [SEQ // 128, 128, NH * 65], BF16,
                            kind="Internal").ap()
    s_rc = nc.dram_tensor("s_rc", [8, 2 * QB], BF16, kind="Internal").ap()

    es = ExitStack()
    w_sb = es.enter_context(nc.sbuf_tensor("w_sb", [128, 3 * NCH * D], BF16))
    ones_bf = es.enter_context(nc.sbuf_tensor("ones_bf", [128, 128], BF16))
    bv_row = es.enter_context(nc.sbuf_tensor("bv_row", [128, D], BF16))
    smc = es.enter_context(nc.sbuf_tensor("smc", [128, 20 * NCH], F32))
    SM = {n: i for i, n in enumerate(
        ["bq", "bk", "mur", "muk", "murm1", "mukm1", "znq", "znk",
         "cqq", "sqq", "tzq", "cqk", "sqk", "tzk", "tmp0", "tmp1",
         "x1prev", "cone", "nsqq", "nsqk"])}
    lnst = es.enter_context(nc.sbuf_tensor("lnst", [128, 80], F32))
    LNI = {n: i for i, n in enumerate(
        ["sum", "ssq", "m", "var", "inv", "nminv", "minv", "t0"])}
    dump = es.enter_context(nc.sbuf_tensor("dump", [128, D], BF16))
    dump2 = es.enter_context(nc.sbuf_tensor("dump2", [128, D], BF16))

    def smcol(name, i=None):
        j = SM[name]
        a = smc[:, j * NCH:(j + 1) * NCH]
        return a if i is None else a[:, i:i + 1]

    def lncol(name, tt=None, alt=False):
        i = LNI[name] + (10 if alt else 0)
        a = lnst[:, i * 4:(i + 1) * 4]
        return a if tt is None else a[:, tt:tt + 1]

    POOL_BYTES = 140 * 1024
    pool_t = es.enter_context(
        nc.sbuf_tensor("pool", [128, POOL_BYTES // 4], F32))
    pool = Pool(pool_t, POOL_BYTES)

    x_sb = pool.alloc(4 * D, F32)
    Aq2 = [pool.alloc(NCH * QB, BF16),
           pool.alloc(NCH * QB, BF16)]
    Ak2 = [pool.alloc(NCH * QB, BF16),
           pool.alloc(NCH * QB, BF16)]
    Bh4 = [pool.alloc(NCH * QB // 2, BF16) for _ in range(4)]
    Cbh4 = [pool.alloc(NCH * QB // 2, BF16) for _ in range(4)]
    xn_tm2 = [pool.alloc(4 * D, BF16), pool.alloc(4 * D, BF16)]
    xn_fm2 = [pool.alloc(NCH * QB, BF16), pool.alloc(NCH * QB, BF16)]
    xn2 = dump2[:, :]
    lam_b2 = [pool.alloc(QB, BF16), pool.alloc(QB, BF16)]
    Dq = pool.alloc(QB, F32)
    qs_b = Dq
    Qqb = pool.alloc(NCH * QB, BF16)
    Kqb = pool.alloc(NCH * QB, BF16)
    v_st = pool.alloc(2 * NH * 65, BF16)
    prod_end = pool.off

    pool.off = 0
    x2_sb = pool.alloc(2 * D, F32)
    x1_sb = pool.alloc(4 * D, F32)
    Ksl = pool.alloc(2 * SEQ, BF16)
    Qsl = pool.alloc(2 * QB, BF16)
    vsl = pool.alloc(2 * (16 * 130 + 64), BF16)
    PT = pool.alloc(4 * 1024, BF16)
    attn_fm2 = [pool.alloc(NCH * QB, BF16),
                pool.alloc(NCH * QB, BF16)]
    attn_tm = pool.alloc(4 * D, BF16, "attn_tm")
    rcb = pool.alloc(2 * QB, BF16)
    den_sb = pool.alloc(QB, BF16)
    den_tm = pool.alloc(4 * NH, BF16)
    rc_tm = pool.alloc(4 * NH, F32)
    r_fm = pool.alloc(NCH * QB, BF16)
    of_fm = pool.alloc(NCH * QB, BF16)
    of_tm = pool.view(pool.marks["attn_tm"], 4 * D, BF16)
    sgE = pool.alloc(QB, F32)
    k0b = pool.alloc(QB, BF16)
    x1n_fm = pool.alloc(NCH * QB, BF16, "x1n_fm")
    x1n_tm = pool.alloc(4 * D, BF16)
    xr_fm = pool.alloc(NCH * QB, BF16)
    xk_fm = pool.alloc(NCH * QB, BF16)
    out_stage = pool.alloc(2 * D, F32)
    attn_end = pool.off
    ka_fm = pool.view(pool.marks["x1n_fm"], NCH * QB, BF16)
    assert max(prod_end, attn_end) <= POOL_BYTES, (prod_end, attn_end)

    psc = es.enter_context(nc.psum_tensor("psc", [128, 3072], F32))
    p_attA = es.enter_context(nc.psum_tensor("p_attA", [128, 512], F32))
    p_attB = es.enter_context(nc.psum_tensor("p_attB", [128, 512], F32))
    p_poin = [psc[:, 0:512], psc[:, 512:1024], p_attA[:, :], p_attB[:, :]]
    p_v = [psc[:, 1024:1536], psc[:, 1536:2048]]
    p_lam = psc[:, 2048:2560]
    p_sw = psc[:, 2560:3072]
    p_ff = [psc[:, 2048:2560], psc[:, 2560:3072]]

    sems = {}

    def w_slot(wi, ci, lo, hi):
        o = (wi * NCH + ci) * D
        return w_sb[:, o + lo:o + hi]

    def fchunk(buf, ci):
        a = buf.rearrange("p (tt c i) -> p tt c i", tt=4, c=NCH)
        return a[:, :, ci:ci + 1, :].squeeze(2)

    def T(fn, waits=(), inc=0):
        return p.emit("t", fn, waits, inc)

    def V(fn, waits=(), inc=0):
        return p.emit("v", fn, waits, inc)

    def Sc(fn, waits=(), inc=0):
        return p.emit("s", fn, waits, inc)

    def Y(fn, waits=(), inc=0, sem=None):
        return p.emit("y", fn, waits, inc, sem)

    def G(fn, waits=(), inc=0, sem=None):
        return p.emit("g", fn, waits, inc, sem)

    Sc(lambda s: s.add_instruction(mybir.InstLoadActFuncSet(
        name=nc.get_next_instruction_name(), act_func_set_id=ACT_SET,
        ins=[], outs=[])), inc=1)
    G(lambda g: g.dma_start(out=ones_bf[:], in_=ones_in), inc=16, sem="g_w")
    G(lambda g: g.dma_start(out=bv_row[:], in_=bv_in), inc=16, sem="g_w")
    for n in ["bq", "bk", "mur", "muk"]:
        G(lambda g, n=n: g.dma_start(out=smcol(n), in_=pp_in[n]),
          inc=16, sem="g_w")
    tk_start = None
    for i, n in enumerate(["zqT", "zkT"]):
        tk_start = G(lambda g, n=n, i=i: g.dma_start(
            out=r3(w_sb[:, (i + 1) * NCH * D:(i + 2) * NCH * D], D),
            in_=w_in[n].rearrange("(c p) d -> p c d", p=128)),
            inc=16, sem="g_w")
    tk_ones = tk_bv = tk_start
    tk_pp = {n: tk_start for n in ["bq", "bk", "mur", "muk"]}
    tk_zt = {n: tk_start for n in ["zqT", "zkT"]}
    V(lambda v: v.tensor_copy(smcol("cone", 0), ones_bf[:, 0:1]),
      waits=[tk_start], inc=1)
    for n, wi, dst in [("zqT", 1, "znq"), ("zkT", 2, "znk")]:
        for ci in range(NCH):
            V(lambda v, wi=wi, ci=ci, dst=dst: v.scalar_tensor_tensor(
                dump[:], w_slot(wi, ci, 0, D), 1.0, w_slot(wi, ci, 0, D),
                ALU.mult, ALU.mult, accum_out=smcol(dst, ci)),
              waits=[tk_zt[n]], inc=1)
    for dst in ["znq", "znk"]:
        Sc(lambda s, dst=dst: s.activation(smcol(dst), smcol(dst), AF.Ln),
           waits=[p.last["v"]], inc=1)
        Sc(lambda s, dst=dst: s.activation(smcol(dst), smcol(dst), AF.Exp,
                                           scale=0.5), inc=1)
        V(lambda v, dst=dst: v.tensor_scalar(
            smcol(dst), smcol(dst), 1e-15, None, ALU.max),
          waits=[p.last["s"]], inc=1)
    for src, dst in [("mur", "murm1"), ("muk", "mukm1")]:
        V(lambda v, src=src, dst=dst: v.tensor_scalar(
            smcol(dst), smcol(src), -1.0, 1.0, ALU.mult, ALU.add),
          waits=[tk_pp[src]], inc=1)
    for b, zn, cq, sq, tz, nsq in [
            ("bq", "znq", "cqq", "sqq", "tzq", "nsqq"),
            ("bk", "znk", "cqk", "sqk", "tzk", "nsqk")]:
        Sc(lambda s, b=b: s.activation(smcol("tmp0"), smcol(b), AF.Exp,
                                       scale=2.0),
           waits=[tk_pp[b], p.last.get("v")], inc=1)
        Sc(lambda s, b=b: s.activation(smcol("tmp1"), smcol(b), AF.Exp,
                                       scale=-2.0), inc=1)
        tv = p.last["s"]
        V(lambda v, zn=zn, tz=tz: v.tensor_scalar(
            smcol(tz), smcol(zn), 2.0, None, ALU.mult), waits=[tv], inc=1)
        V(lambda v, sq=sq: v.tensor_sub(smcol(sq), smcol("tmp0"),
                                        smcol("tmp1")), inc=1)
        V(lambda v, sq=sq: v.tensor_scalar(smcol(sq), smcol(sq), 0.5, None,
                                           ALU.mult), inc=1)
        V(lambda v, sq=sq, nsq=nsq: v.tensor_scalar(
            smcol(nsq), smcol(sq), -1.0, None, ALU.mult), inc=1)
        V(lambda v: v.tensor_add(smcol("tmp0"), smcol("tmp0"),
                                 smcol("tmp1")), inc=1)
        V(lambda v: v.tensor_scalar(smcol("tmp0"), smcol("tmp0"), 0.5, None,
                                    ALU.mult), inc=1)
        V(lambda v, zn=zn: v.reciprocal(smcol("tmp1"), smcol(zn)), inc=1)
        V(lambda v, cq=cq: v.tensor_mul(smcol(cq), smcol("tmp0"),
                                        smcol("tmp1")), inc=1)

    def load_wset(names, wait_tok):
        tk = None
        for i, n in enumerate(names):
            tk = G(lambda g, n=n, i=i: g.dma_start(
                out=r3(w_sb[:, i * NCH * D:(i + 1) * NCH * D], D),
                in_=w_in[n].rearrange("(c p) d -> p c d", p=128)),
                waits=[wait_tok], inc=16, sem="g_w")
        return tk

    tk_wset = load_wset(["zq", "zk", "wvT"], p.last["v"])
    p.barrier()

    def ln_stats(src_tile, wtoks):
        for tt in range(4):
            Sc(lambda s, tt=tt: s.activation(
                dump[:], src_tile(tt), AF.Identity,
                accum_out=lncol("sum", tt)), waits=[wtoks[tt]], inc=1)
            V(lambda v, tt=tt: v.scalar_tensor_tensor(
                dump2[:], src_tile(tt), 1.0, src_tile(tt),
                ALU.mult, ALU.mult, accum_out=lncol("ssq", tt)),
              waits=[wtoks[tt]], inc=1)
        tS = p.last["s"]
        V(lambda v: v.tensor_scalar(lncol("m"), lncol("sum"), 1.0 / D, None,
                                    ALU.mult), waits=[tS], inc=1)
        V(lambda v: v.tensor_mul(lncol("t0"), lncol("m"), lncol("m")), inc=1)
        V(lambda v: v.tensor_scalar(lncol("t0"), lncol("t0"), EPS, None,
                                    ALU.subtract), inc=1)
        V(lambda v: v.scalar_tensor_tensor(
            lncol("var"), lncol("ssq"), 1.0 / D, lncol("t0"),
            ALU.mult, ALU.subtract), inc=1)
        tV = p.last["v"]
        Sc(lambda s: s.activation(lncol("inv"), lncol("var"), AF.Ln),
           waits=[tV], inc=1)
        Sc(lambda s: s.activation(lncol("inv"), lncol("inv"), AF.Exp,
                                  scale=-0.5), inc=1)
        tS2 = p.last["s"]
        V(lambda v: v.tensor_mul(lncol("minv"), lncol("m"), lncol("inv")),
          waits=[tS2], inc=1)
        V(lambda v: v.tensor_scalar(lncol("nminv"), lncol("minv"), -1.0, None,
                                    ALU.mult), inc=1)
        return p.last["v"]

    def ln_stats_v(src_tile, wtoks):
        for tt in range(4):
            V(lambda v, tt=tt: v.tensor_scalar(
                dump[:], src_tile(tt), 1.0, 0.0, ALU.mult, ALU.add,
                accum_out=lncol("sum", tt)), waits=[wtoks[tt]], inc=1)
            V(lambda v, tt=tt: v.scalar_tensor_tensor(
                dump2[:], src_tile(tt), 1.0, src_tile(tt),
                ALU.mult, ALU.mult, accum_out=lncol("ssq", tt)), inc=1)
        V(lambda v: v.tensor_scalar(lncol("m"), lncol("sum"), 1.0 / D, None,
                                    ALU.mult), inc=1)
        V(lambda v: v.tensor_mul(lncol("t0"), lncol("m"), lncol("m")), inc=1)
        V(lambda v: v.tensor_scalar(lncol("t0"), lncol("t0"), EPS, None,
                                    ALU.subtract), inc=1)
        tV = V(lambda v: v.scalar_tensor_tensor(
            lncol("var"), lncol("ssq"), 1.0 / D, lncol("t0"),
            ALU.mult, ALU.subtract), inc=1)
        Sc(lambda s: s.activation(lncol("inv"), lncol("var"), AF.Ln),
           waits=[tV], inc=1)
        tS2 = Sc(lambda s: s.activation(lncol("inv"), lncol("inv"), AF.Exp,
                                        scale=-0.5), inc=1)
        V(lambda v: v.tensor_mul(lncol("minv"), lncol("m"), lncol("inv")),
          waits=[tS2], inc=1)
        return V(lambda v: v.tensor_scalar(lncol("nminv"), lncol("minv"),
                                           -1.0, None, ALU.mult), inc=1)

    def lnv_red(src_tile, wtoks, alt=False):
        for tt in range(4):
            V(lambda v, tt=tt: v.tensor_scalar(
                dump[:], src_tile(tt), 1.0, 0.0, ALU.mult, ALU.add,
                accum_out=lncol("sum", tt, alt)), waits=[wtoks[tt]], inc=1)
            V(lambda v, tt=tt: v.scalar_tensor_tensor(
                dump2[:], src_tile(tt), 1.0, src_tile(tt),
                ALU.mult, ALU.mult, accum_out=lncol("ssq", tt, alt)),
              inc=1)
        V(lambda v: v.tensor_scalar(
            lncol("m", None, alt), lncol("sum", None, alt), 1.0 / D, None,
            ALU.mult), inc=1)
        V(lambda v: v.tensor_mul(lncol("t0", None, alt),
                                 lncol("m", None, alt),
                                 lncol("m", None, alt)), inc=1)
        V(lambda v: v.tensor_scalar(
            lncol("t0", None, alt), lncol("t0", None, alt), EPS, None,
            ALU.subtract), inc=1)
        return V(lambda v: v.scalar_tensor_tensor(
            lncol("var", None, alt), lncol("ssq", None, alt), 1.0 / D,
            lncol("t0", None, alt), ALU.mult, ALU.subtract), inc=1)

    def lnv_inv(tV, alt=False):
        Sc(lambda s: s.activation(lncol("inv", None, alt),
                                  lncol("var", None, alt), AF.Ln),
           waits=[tV], inc=1)
        return Sc(lambda s: s.activation(lncol("inv", None, alt),
                                         lncol("inv", None, alt), AF.Exp,
                                         scale=-0.5), inc=1)

    def lnv_fin(tS, alt=False):
        V(lambda v: v.tensor_mul(lncol("minv", None, alt),
                                 lncol("m", None, alt),
                                 lncol("inv", None, alt)),
          waits=[tS], inc=1)
        return V(lambda v: v.tensor_scalar(
            lncol("nminv", None, alt), lncol("minv", None, alt),
            -1.0, None, ALU.mult), inc=1)

    tk_vst_dma = [None, None]
    for s_i in range(NSEQ):
        pc = {"hev": [None] * 4, "lam_free": None, "x_cons": [],
              "lam_mm": [None, None],
              "fin": {(w, r): None for w in "qk" for r in (0, 1)},
              "bh_free": [None] * 4, "cb": [None] * 4,
              "head": None, "xnf_free": [None, None], "vev": [None, None]}

        def emit_head1(qb):
            par = qb % 2
            base = qb * QB
            xnf = xn_fm2[par]
            xntm = xn_tm2[par]
            tk_x = G(lambda g, s_i=s_i, base=base: g.dma_start(
                out=r3(x_sb, D),
                in_=x_in[s_i, base:base + QB, :].rearrange(
                    "(t p) d -> p t d", p=128)),
                waits=pc["x_cons"], inc=16, sem="g_x")
            tk_ln = ln_stats(lambda tt: x_sb[:, tt * D:(tt + 1) * D],
                             [tk_x] * 4)
            tk_xn = []
            for tt in range(4):
                tk_xn.append(V(lambda v, tt=tt, xntm=xntm: v.tensor_scalar(
                    xntm[:, tt * D:(tt + 1) * D],
                    x_sb[:, tt * D:(tt + 1) * D],
                    lncol("inv", tt), lncol("nminv", tt),
                    ALU.mult, ALU.add),
                    waits=[tk_ln, pc.get("xtm_free_%d" % par)], inc=1))
            tS_ln = p.last["s"]
            tk_xf_all = Y(lambda y, xnf=xnf, xntm=xntm:
                          y.dma_start_transpose(
                r3(xnf[:, :]), r3(xntm[:, :])),
                waits=[tk_xn[3], pc["xnf_free"][par]], inc=16,
                sem="y_xp")
            pc["xtm_free_%d" % par] = tk_xf_all
            return {"xf_all": tk_xf_all, "par": par, "base": base,
                    "xnf": xnf, "tk_xn3": tk_xn[3], "tS_ln": tS_ln}

        def emit_head2(hd):
            par = hd["par"]
            xnf = hd["xnf"]
            lamb = lam_b2[par]
            tk_xf_all = hd["xf_all"]
            tk_x2c = [None] * NCH
            tk_lmm = [None] * NCH
            for ci in range(NCH):
                tk_x2c[ci] = V(lambda v, ci=ci, xnf=xnf: v.tensor_mul(
                    r3(xn2[:, (ci % 2) * QB:(ci % 2 + 1) * QB]),
                    fchunk(xnf, ci), fchunk(xnf, ci)),
                    waits=[tk_xf_all,
                           tk_lmm[ci - 2] if ci >= 2 else
                           pc["lam_mm"][ci]], inc=1)
                tk_lmm[ci] = T(lambda t, ci=ci: t.matmul(
                    p_lam, ones_bf[:],
                    xn2[:, (ci % 2) * QB:(ci % 2 + 1) * QB],
                    start=(ci == 0), stop=(ci == NCH - 1),
                    skip_group_check=True),
                    waits=[tk_x2c[ci], tk_ones, pc["lam_free"]], inc=1)
            pc["lam_mm"] = [tk_lmm[6], tk_lmm[7]]
            tkv = V(lambda v, lamb=lamb: v.tensor_scalar(
                lamb[:], p_lam, 1.0, None, ALU.subtract),
                waits=[tk_lmm[NCH - 1]], inc=1)
            pc["lam_free"] = tkv
            Sc(lambda s, lamb=lamb: s.activation(lamb[:], lamb[:], AF.Ln),
               waits=[tkv], inc=1)
            tks = Sc(lambda s, lamb=lamb: s.activation(
                lamb[:], lamb[:], AF.Exp, scale=-1.0), inc=1)
            tk_lam = V(lambda v, lamb=lamb: v.tensor_scalar(
                lamb[:], lamb[:], -2.0, None, ALU.mult),
                waits=[tks], inc=1)
            pc["x_cons"] = [hd["tk_xn3"], hd["tS_ln"], tk_lmm[NCH - 1]]
            hd["lam"] = tk_lam
            return hd

        def emit_head(qb):
            return emit_head2(emit_head1(qb))

        def tails_mk(bnd):
            HB = NCH * QB // 2
            units = []
            for wi, (which, AA) in enumerate([("q", bnd["Aq"]),
                                              ("k", bnd["Ak"])]):
                for h in (0, 1):
                    units.append({
                        "u": wi * 2 + h, "which": which, "AA": AA,
                        "h": h, "Ah": AA[:, h * HB:(h + 1) * HB],
                        "tz": "tzq" if which == "q" else "tzk",
                        "tk": bnd["hev"][wi * NCH + h * 4 + 3],
                    })
            return {"bnd": bnd, "units": units}

        def tails_s1(st, idx=None):
            for un in (st["units"] if idx is None
                       else [st["units"][idx]]):
                un["tk"] = V(lambda v, un=un: v.tensor_mul(
                    Bh4[un["u"]][:], un["Ah"], un["Ah"]),
                    waits=[un["tk"], pc["bh_free"][un["u"]]], inc=1)

        def tails_s23(st, idx=None):
            for un in (st["units"] if idx is None
                       else [st["units"][idx]]):
                Sc(lambda s, un=un: s.activation(
                    Bh4[un["u"]][:], Bh4[un["u"]][:], AF.Ln,
                    bias=smcol("cone", 0)), waits=[un["tk"]], inc=1)
                un["tk"] = Sc(lambda s, un=un: s.activation(
                    Bh4[un["u"]][:], Bh4[un["u"]][:], AF.Exp, scale=0.5),
                    inc=1)

        def tails_s4(st, idx=None):
            for un in (st["units"] if idx is None
                       else [st["units"][idx]]):
                un["tk"] = V(lambda v, un=un: v.tensor_add(
                    un["Ah"], un["Ah"], Bh4[un["u"]][:]),
                    waits=[un["tk"]], inc=1)

        def tails_s5(st):
            for un in st["units"]:
                un["tk"] = Sc(lambda s, un=un: s.activation(
                    un["Ah"], un["Ah"], AF.Ln), waits=[un["tk"]], inc=1)

        def tails_s678(st):
            for un in st["units"]:
                for dc in range(un["h"] * 4, un["h"] * 4 + 4):
                    un["tk"] = V(lambda v, un=un, dc=dc: v.tensor_scalar(
                        un["AA"][:, dc * QB:(dc + 1) * QB],
                        un["AA"][:, dc * QB:(dc + 1) * QB],
                        smcol(un["tz"], dc), None, ALU.mult),
                        waits=[un["tk"]], inc=1)
                Sc(lambda s, un=un: s.activation(Bh4[un["u"]][:],
                                                 un["Ah"], AF.Exp),
                   waits=[un["tk"]], inc=1)
                un["tk"] = Sc(lambda s, un=un: s.activation(
                    un["Ah"], un["Ah"], AF.Exp, scale=-1.0), inc=1)
            for un in st["units"]:
                un["tk"] = V(lambda v, un=un: v.tensor_sub(
                    un["Ah"], Bh4[un["u"]][:], un["Ah"]),
                    waits=[un["tk"]], inc=1)
                pc["bh_free"][un["u"]] = un["tk"]

        def tails_fin(st):
            bnd = st["bnd"]
            base = bnd["base"]
            par = bnd["par"]
            units = st["units"]
            for wi, (which, outqb, store) in enumerate(
                    [("q", Qqb, s_qfm), ("k", Kqb, s_kfm)]):
                AA = bnd["Aq"] if which == "q" else bnd["Ak"]
                uns = units[wi * 2:wi * 2 + 2]
                tk_w2 = [None, None]
                for h in (0, 1):
                    un = uns[h]
                    tk_w2[h] = V(lambda v, un=un: v.tensor_mul(
                        Cbh4[un["u"]][:], un["Ah"], un["Ah"]),
                        waits=[un["tk"], pc["cb"][un["u"]]], inc=1)
                tmm = None
                for h in (0, 1):
                    for j in range(4):
                        dc = h * 4 + j
                        u = wi * 2 + h
                        tmm = T(lambda t, u=u, j=j, dc=dc: t.matmul(
                            p_sw, ones_bf[:],
                            Cbh4[u][:, j * QB:(j + 1) * QB],
                            start=(dc == 0), stop=(dc == NCH - 1),
                            skip_group_check=True),
                            waits=[tk_w2[h]], inc=(1 if j == 3 else 0))
                    pc["cb"][wi * 2 + h] = tmm
                V(lambda v: v.tensor_scalar(Dq[:], p_sw, 0.25, 1.0,
                                            ALU.mult, ALU.add),
                  waits=[tmm], inc=1)
                tV = p.last["v"]
                Sc(lambda s: s.activation(Dq[:], Dq[:], AF.Ln), waits=[tV],
                   inc=1)
                Sc(lambda s: s.activation(Dq[:], Dq[:], AF.Exp, scale=0.5),
                   inc=1)
                tS = p.last["s"]
                tkv = V(lambda v: v.tensor_scalar(Dq[:], Dq[:], 1.0, None,
                                                  ALU.add), waits=[tS],
                        inc=1)
                Sc(lambda s: s.activation(qs_b[:], Dq[:], AF.Ln),
                   waits=[tkv], inc=1)
                tks = Sc(lambda s: s.activation(qs_b[:], qs_b[:], AF.Exp,
                                                scale=-1.0), inc=1)
                qs_bf = Bh4[wi * 2][:, 0:QB]
                V(lambda v, qs_bf=qs_bf: v.tensor_scalar(
                    qs_bf, qs_b[:], 0.5, None, ALU.mult),
                  waits=[tks], inc=1)
                out4 = outqb.rearrange("p (tt c i) -> p c tt i",
                                       tt=4, c=NCH)
                a4 = AA.rearrange("p (c tt i) -> p c tt i", c=NCH, tt=4)
                q4 = qs_bf.rearrange("p (tt i) -> p tt i", tt=4) \
                    .unsqueeze(1).broadcast_to([128, NCH, 4, 128])
                tk_fin = V(lambda v, out4=out4, a4=a4, q4=q4:
                           v.tensor_tensor(out4, a4, q4, ALU.mult),
                           waits=[("g_qkst", p.cnt["g_qkst"])], inc=1)
                pc["fin"][(which, par)] = tk_fin
                pc["bh_free"][wi * 2] = tk_fin
                for tt in range(4):
                    G(lambda g, outqb=outqb, store=store, base=base, tt=tt:
                      g.dma_start(
                          out=store[:, :, base + tt * 128:
                                    base + (tt + 1) * 128].rearrange(
                              "c p i -> p c i"),
                          in_=r3(outqb[:, tt * D:(tt + 1) * D])),
                      waits=[tk_fin], inc=16, sem="g_qkst")

        def emit_tails(bnd):
            st = tails_mk(bnd)
            tails_s1(st)
            tails_s23(st)
            tails_s4(st)
            tails_s5(st)
            tails_s678(st)
            tails_fin(st)

        bnd_p = None
        hd = emit_head(0)
        for qb in range(NQB):
            base = hd["base"]
            par = hd["par"]
            xnf = hd["xnf"]
            lamb = lam_b2[par]
            tk_xf_all = hd["xf_all"]
            tk_lam = hd["lam"]
            hev_toks = {}
            hev_bank = {}

            def emit_proj(which, wslot, filler=None, par=par, xnf=xnf,
                          lamb=lamb, tk_xf_all=tk_xf_all, tk_lam=tk_lam):
                cq, sq = (("cqq", "sqq") if which == "q"
                          else ("cqk", "sqk"))
                for dc in range(NCH):
                    gidx = (0 if which == "q" else NCH) + dc
                    bank = gidx % 4
                    tk_pj = None
                    for ci in range(NCH):
                        tk_pj = T(lambda t, dc=dc, ci=ci, wslot=wslot,
                                  bank=bank, xnf=xnf: t.matmul(
                            p_poin[bank],
                            w_slot(wslot, ci, dc * 128, (dc + 1) * 128),
                            fchunk(xnf, ci),
                            start=(ci == 0), stop=(ci == NCH - 1),
                            skip_group_check=True),
                            waits=[tk_xf_all, tk_wset,
                                   hev_bank.get(gidx - 4, pc["hev"][gidx])
                                   if gidx < 4 else hev_bank.get(gidx - 4)],
                            inc=(1 if ci == NCH - 1 else 0))
                    AA = Aq2[par] if which == "q" else Ak2[par]
                    nsq = "nsqq" if which == "q" else "nsqk"
                    tk_h = Sc(lambda s, dc=dc, cq=cq, nsq=nsq, bank=bank,
                              AA=AA: s.activation(
                        AA[:, dc * QB:(dc + 1) * QB], p_poin[bank],
                        AF.Identity, scale=smcol(cq, dc),
                        bias=smcol(nsq, dc)),
                        waits=[tk_pj, pc["fin"][(which, par)]], inc=1)
                    hev_bank[gidx] = tk_h
                    V(lambda v, dc=dc, AA=AA, lamb=lamb: v.tensor_mul(
                        AA[:, dc * QB:(dc + 1) * QB],
                        AA[:, dc * QB:(dc + 1) * QB], lamb[:]),
                      waits=[tk_lam, tk_h], inc=1)
                    hev_toks[gidx] = V(lambda v, dc=dc, sq=sq, AA=AA:
                                       v.tensor_scalar(
                        AA[:, dc * QB:(dc + 1) * QB],
                        AA[:, dc * QB:(dc + 1) * QB],
                        smcol(sq, dc), None, ALU.add), inc=1)
                    if filler is not None:
                        filler(dc)

            def emit_vproj(tts, qb=qb, xnf=xnf, tk_xf_all=tk_xf_all):
                for tt in tts:
                    tk_vp = [None, None]
                    for hf in range(2):
                        for ci in range(NCH):
                            tk_vp[hf] = T(lambda t, tt=tt, hf=hf, ci=ci,
                                          xnf=xnf: t.matmul(
                                p_v[hf],
                                xnf[:, tt * D + ci * 128:
                                    tt * D + (ci + 1) * 128],
                                w_slot(2, ci, hf * 512, (hf + 1) * 512),
                                start=(ci == 0), stop=(ci == NCH - 1),
                                skip_group_check=True),
                                waits=[tk_xf_all, tk_vst_dma[tt % 2],
                                       pc["vev"][hf]],
                                inc=(1 if ci == NCH - 1 else 0))
                    vst = v_st[:, (tt % 2) * NH * 65:
                               (tt % 2 + 1) * NH * 65].rearrange(
                        "p (h x) -> p h x", x=65)
                    for hf in range(2):
                        pc["vev"][hf] = V(lambda v, hf=hf, vst=vst:
                                          v.tensor_tensor(
                            vst[:, hf * 8:(hf + 1) * 8, 0:64],
                            p_v[hf].rearrange("p (h x) -> p h x", x=64),
                            bv_row[:, hf * 512:(hf + 1) * 512].rearrange(
                                "p (h x) -> p h x", x=64), ALU.add),
                            waits=[tk_vp[hf], tk_bv], inc=1)
                    tk_pk = V(lambda v, vst=vst: v.tensor_copy(
                        vst[:, :, 64:65], ones_bf[:, 0:16].unsqueeze(2)),
                        waits=[tk_ones], inc=1)
                    tk_vst_dma[tt % 2] = G(
                        lambda g, tt=tt, qb=qb: g.dma_start(
                            out=s_vaug[qb * 4 + tt, :, :],
                            in_=v_st[:, (tt % 2) * NH * 65:
                                     (tt % 2 + 1) * NH * 65]),
                        waits=[tk_pk], inc=16, sem="g_vst%d" % (tt % 2))

            st = tails_mk(bnd_p) if bnd_p is not None else None

            def fil_q(dc, st=st):
                if st is not None and dc % 2 == 1:
                    tails_s1(st, dc // 2)

            def fil_k(dc, st=st):
                if st is None:
                    return
                i = dc // 2
                if dc % 2 == 1:
                    if i >= 1:
                        tails_s4(st, i - 1)
                    tails_s23(st, i)

            emit_proj("q", 0, fil_q)
            emit_proj("k", 1, fil_k)
            if st:
                tails_s4(st, 3)
                tails_s5(st)
                tails_s678(st)
            hd1 = emit_head1(qb + 1) if qb + 1 < NQB else None
            pc["hev"] = [hev_bank[12], hev_bank[13], hev_bank[14],
                         hev_bank[15]]
            emit_vproj((0, 1))
            if st:
                tails_fin(st)
            if hd1 is not None:
                hd = emit_head2(hd1)
            emit_vproj((2, 3))
            pc["xnf_free"][par] = p.last["t"]
            bnd_p = {"qb": qb, "base": base, "hev": hev_toks,
                     "Aq": Aq2[par], "Ak": Ak2[par], "par": par}
        tk_wset = load_wset(["wrT", "wkT", "wfT"], p.last["t"])
        emit_tails(bnd_p)
        bnd_p = None
        p.barrier()

        cy = {
            "qkl": [None, None], "pv": [None, None], "evac": None,
            "exp": {}, "pvg": {}, "rc": None, "dn": None, "ofx": None,
            "afm_free": [None, None], "x1stt": None, "x1ntm_free": None,
            "x2c": [None, None],
            "mix": None, "mix_r": None, "v2": None,
            "ffn_pu": [None, None],
            "of_free": None, "odma": [None, None], "rfree": None,
            "kafree": None, "xrkfree": None,
        }

        def emit_x2q(qb, tt, wait):
            return G(lambda g, tt=tt, base=qb * QB, s_i=s_i: g.dma_start(
                out=x2_sb[:, (tt % 2) * D:(tt % 2 + 1) * D],
                in_=x_in[s_i, base + tt * 128:base + (tt + 1) * 128, :]),
                waits=[wait], inc=16,
                sem=("g_x2a" if tt % 2 == 0 else "g_x2b"))

        def emit_bndA_y(bnd):
            tk_dn = Y(lambda y: y.dma_start(
                out=den_sb[0:16, :],
                in_=s_rc[:, :].rearrange("a (b j) -> (a b) j", b=2)),
                waits=[("g_rc", p.cnt["g_rc"]), cy.get("rcp")],
                inc=16, sem="y_dn")
            cy["dn"] = tk_dn
            tk_xa = Y(lambda y, afm=attn_fm2[bnd["qb"] % 2]:
                      y.dma_start_transpose(
                r3(attn_tm[:, :]), r3(afm[:, :])),
                waits=[bnd["evac7"], cy["x1stt"], cy["of_free"]],
                inc=16, sem="y_xp")
            cy["afm_free"][bnd["qb"] % 2] = tk_xa
            tk_dt = Y(lambda y: y.dma_start_transpose(
                den_tm[:, :].rearrange("p (b t) -> p b t", t=16),
                den_sb[0:16, :].rearrange("p (b t) -> p b t", t=128)),
                waits=[tk_dn], inc=16, sem="y_xp")
            bnd["xa"] = tk_xa
            bnd["dt"] = tk_dt

        def emit_bndA_v(bnd):
            tk_rcp = V(lambda v: v.reciprocal(rc_tm[:], den_tm[:]),
                       waits=[bnd["dt"]], inc=1)
            cy["rcp"] = tk_rcp
            tk_div = []
            for tt in range(4):
                rc3 = rc_tm[:, tt * 16:(tt + 1) * 16].unsqueeze(2) \
                    .broadcast_to([128, 16, 64])
                av = attn_tm[:, tt * D:(tt + 1) * D].rearrange(
                    "p (h x) -> p h x", x=64)
                tk_div.append(V(lambda v, av=av, rc3=rc3: v.tensor_tensor(
                    av, av, rc3, ALU.mult),
                    waits=[bnd["xa"], tk_rcp], inc=1))
            bnd["tV2"] = lnv_red(
                lambda tt: attn_tm[:, tt * D:(tt + 1) * D],
                [tk_div[tt] for tt in range(4)])

        def emit_bndB(bnd):
            tS2 = lnv_inv(bnd["tV2"])
            tfin = lnv_fin(tS2)
            tk_x2t = bnd["x2t"]
            tk_x1 = []
            for tt in range(4):
                slo = (tt % 2) * D
                V(lambda v, tt=tt, slo=slo: v.tensor_scalar(
                    out_stage[:, slo:slo + D], x2_sb[:, slo:slo + D],
                    lncol("minv", tt), None, ALU.subtract),
                  waits=[tfin, tk_x2t[tt], cy["odma"][tt % 2]], inc=1)
                tk_x1.append(V(lambda v, tt=tt, slo=slo:
                               v.scalar_tensor_tensor(
                    x1_sb[:, tt * D:(tt + 1) * D],
                    attn_tm[:, tt * D:(tt + 1) * D],
                    lncol("inv", tt), out_stage[:, slo:slo + D],
                    ALU.mult, ALU.add), inc=1))
                if tt + 2 < 4:
                    tk_x2t[tt + 2] = emit_x2q(bnd["qb"], tt + 2, tk_x1[tt])
            cy["x2c"] = [tk_x1[2], tk_x1[3]]
            cy["x1stt"] = tk_x1[3]
            bnd["tV3"] = lnv_red(lambda tt: x1_sb[:, tt * D:(tt + 1) * D],
                                 tk_x1)

        def emit_bndC(bnd):
            tS3 = lnv_inv(bnd["tV3"])
            tfin3 = lnv_fin(tS3)
            tk_x1n = []
            for tt in range(4):
                tk_x1n.append(V(lambda v, tt=tt: v.tensor_scalar(
                    x1n_tm[:, tt * D:(tt + 1) * D],
                    x1_sb[:, tt * D:(tt + 1) * D],
                    lncol("inv", tt), lncol("nminv", tt),
                    ALU.mult, ALU.add),
                    waits=[tfin3, cy["x1ntm_free"]], inc=1))
            tk_xnf = Y(lambda y: y.dma_start_transpose(
                r3(x1n_fm[:, :]), r3(x1n_tm[:, :])),
                waits=[tk_x1n[3], cy["v2"]], inc=16, sem="y_xp")
            cy["x1ntm_free"] = tk_xnf
            bnd["xnf_all"] = tk_xnf

        def emit_bndD(bnd):
            tk_xnf_all = bnd["xnf_all"]
            if bnd["qb"] == 0:
                V(lambda v: v.memzero(smcol("x1prev")), inc=1)
            x1n4 = x1n_fm.rearrange("p (tt c i) -> p tt c i", tt=4, c=NCH)
            dv8 = dump.bitcast(F32)[:, 0:8]
            dv24 = dump.bitcast(F32)[:, 8:32].rearrange(
                "p (a b) -> p a b", a=3)
            tk_mix = None
            for dst, mu, mum1, key in [(xr_fm, "mur", "murm1", "mix_r"),
                                       (xk_fm, "muk", "mukm1", "mix")]:
                d4 = dst.rearrange("p (tt c i) -> p tt c i", tt=4, c=NCH)
                for ci in range(NCH):
                    V(lambda v, d4=d4, mu=mu, ci=ci: v.tensor_scalar(
                        d4[:, :, ci:ci + 1, :].squeeze(2),
                        x1n4[:, :, ci:ci + 1, :].squeeze(2),
                        smcol(mu, ci), None, ALU.mult),
                      waits=[tk_xnf_all, cy["xrkfree"]], inc=1)
                    V(lambda v, d4=d4, mum1=mum1, ci=ci:
                      v.scalar_tensor_tensor(
                        d4[:, :, ci:ci + 1, 1:128].squeeze(2),
                        x1n4[:, :, ci:ci + 1, 0:127].squeeze(2),
                        smcol(mum1, ci),
                        d4[:, :, ci:ci + 1, 1:128].squeeze(2),
                        ALU.mult, ALU.add), inc=1)
                b0 = d4[:, 0:1, :, 0:1].squeeze(3).squeeze(1)
                V(lambda v, mum1=mum1: v.tensor_mul(
                    dv8, smcol("x1prev"), smcol(mum1)), inc=1)
                V(lambda v, b0=b0: v.tensor_add(b0, dv8, b0), inc=1)
                src3 = x1n4[:, 0:3, :, 127:128].squeeze(3)
                mb3 = smcol(mum1).unsqueeze(1).broadcast_to([128, 3, NCH])
                V(lambda v, src3=src3, mb3=mb3: v.tensor_mul(
                    dv24, src3, mb3), inc=1)
                b3 = d4[:, 1:4, :, 0:1].squeeze(3)
                tk_mix = V(lambda v, b3=b3: v.tensor_add(b3, dv24, b3),
                           inc=1)
                cy[key] = tk_mix
            prevsrc = x1n4[:, 3:4, :, 127:128].squeeze(3).squeeze(1)
            tk_mix = V(lambda v: v.tensor_copy(smcol("x1prev"), prevsrc),
                       waits=[tk_mix], inc=1)
            cy["mix"] = tk_mix
            cy["kafree"] = tk_xnf_all

        def emit_ffn1R(bnd):
            tk_sg = bnd["sg"] = [None] * NCH
            pu = cy["ffn_pu"]
            for dc in range(NCH):
                tk_rp = None
                for ci in range(NCH):
                    tk_rp = T(lambda t, dc=dc, ci=ci: t.matmul(
                        p_ff[dc % 2],
                        w_slot(0, ci, dc * 128, (dc + 1) * 128),
                        fchunk(xr_fm, ci),
                        start=(ci == 0), stop=(ci == NCH - 1),
                        skip_group_check=True),
                        waits=[cy["mix_r"], tk_wset, pu[dc % 2]],
                        inc=(1 if ci == NCH - 1 else 0))
                tk_e = Sc(lambda s, dc=dc: s.activation(
                    sgE[:], p_ff[dc % 2], AF.Exp, scale=-1.0),
                    waits=[tk_rp, tk_sg[dc - 1] if dc >= 1
                           else cy.get("sgfree")], inc=1)
                Sc(lambda s: s.activation(sgE[:], sgE[:], AF.Ln,
                                          bias=smcol("cone", 0)), inc=1)
                tk_sg[dc] = Sc(lambda s, dc=dc: s.activation(
                    r_fm[:, dc * QB:(dc + 1) * QB], sgE[:], AF.Exp,
                    scale=-1.0), waits=[cy["rfree"]], inc=1)
                pu[dc % 2] = tk_e

        def emit_ffn1K(bnd):
            tk_ka = bnd["ka"] = [None] * NCH
            pu = cy["ffn_pu"]
            for dc in range(NCH):
                tk_kp = None
                for ci in range(NCH):
                    tk_kp = T(lambda t, dc=dc, ci=ci: t.matmul(
                        p_ff[dc % 2],
                        w_slot(1, ci, dc * 128, (dc + 1) * 128),
                        fchunk(xk_fm, ci),
                        start=(ci == 0), stop=(ci == NCH - 1),
                        skip_group_check=True),
                        waits=[cy["mix"], pu[dc % 2]],
                        inc=(1 if ci == NCH - 1 else 0))
                V(lambda v, dc=dc: v.tensor_scalar(
                    k0b[:], p_ff[dc % 2], 0.0, None, ALU.max),
                  waits=[tk_kp, cy["kafree"]], inc=1)
                tk_ka[dc] = V(lambda v, dc=dc: v.tensor_mul(
                    fchunk(ka_fm, dc), r3(k0b[:]), r3(k0b[:])), inc=1)
                pu[dc % 2] = tk_ka[dc]
                if dc == NCH - 1:
                    cy["xrkfree"] = tk_kp

        def emit_ffn1V(bnd):
            tk_sg = bnd["sg"]
            tk_ka = bnd["ka"]
            pu = cy["ffn_pu"]
            tk_of = [None] * NCH
            tk_v2 = None
            for dc in range(NCH):
                for ci in range(NCH):
                    tk_v2 = T(lambda t, dc=dc, ci=ci: t.matmul(
                        p_ff[dc % 2],
                        w_slot(2, ci, dc * 128, (dc + 1) * 128),
                        fchunk(ka_fm, ci),
                        start=(ci == 0), stop=(ci == NCH - 1),
                        skip_group_check=True),
                        waits=[tk_ka[NCH - 1], pu[dc % 2]],
                        inc=(1 if ci == NCH - 1 else 0))
                tk_of[dc] = V(lambda v, dc=dc: v.tensor_tensor(
                    fchunk(of_fm, dc),
                    r3(r_fm[:, dc * QB:(dc + 1) * QB]),
                    r3(p_ff[dc % 2]), ALU.mult),
                    waits=[tk_v2, tk_sg[dc],
                           cy["ofx"] if dc == 0 else None], inc=1)
                pu[dc % 2] = tk_of[dc]
            cy["v2"] = tk_v2
            cy["rfree"] = tk_of[NCH - 1]
            cy["sgfree"] = tk_sg[NCH - 1]
            bnd["of_last"] = tk_of[-1]

        def emit_ffn2(bnd):
            tk_ot = Y(lambda y: y.dma_start_transpose(
                r3(of_tm[:, :]), r3(of_fm[:, :])),
                waits=[bnd["of_last"], cy["of_free"], cy["x1stt"]],
                inc=16, sem="y_xp")
            cy["ofx"] = tk_ot
            bnd["tV4"] = lnv_red(
                lambda tt: of_tm[:, tt * D:(tt + 1) * D],
                [tk_ot] * 4, alt=True)

        def emit_ffn3(bnd):
            tS4 = lnv_inv(bnd["tV4"], alt=True)
            tfin4 = lnv_fin(tS4, alt=True)
            jbase = bnd["base"]
            tk_o = None
            for tt in range(4):
                slo = (tt % 2) * D
                V(lambda v, tt=tt, slo=slo: v.tensor_scalar(
                    out_stage[:, slo:slo + D],
                    x1_sb[:, tt * D:(tt + 1) * D],
                    lncol("minv", tt, True), None, ALU.subtract),
                  waits=[tfin4, cy["odma"][tt % 2]], inc=1)
                tk_o = V(lambda v, tt=tt, slo=slo: v.scalar_tensor_tensor(
                    out_stage[:, slo:slo + D],
                    of_tm[:, tt * D:(tt + 1) * D],
                    lncol("inv", tt, True), out_stage[:, slo:slo + D],
                    ALU.mult, ALU.add), inc=1)
                cy["odma"][tt % 2] = G(lambda g, jbase=jbase, tt=tt,
                                       slo=slo, s_i=s_i: g.dma_start(
                    out=out_d[s_i, jbase + tt * 128:
                              jbase + (tt + 1) * 128, :],
                    in_=out_stage[:, slo:slo + D]), waits=[tk_o], inc=16,
                    sem=("g_out0" if tt % 2 == 0 else "g_out1"))
            cy["of_free"] = tk_o

        pend_b = None
        for qb in range(NQB):
            base = qb * QB
            bnd_cur = {"qb": qb, "base": base, "x2t": [None] * 4}
            tk_qkl = [None] * 8
            tk_pv = [None] * 8
            tk_evac = [None] * 8
            tk_pvg = {}
            tk_exp = {}

            def emit_loads(pr, base=base):
                sl = pr % 2
                sname = "y_l%d" % sl
                wk = [tk_qkl[pr - 2] if pr >= 2 else cy["qkl"][sl]]
                wv_ = [tk_pv[pr - 2] if pr >= 2 else cy["pv"][sl]]
                Y(lambda y, pr=pr, sl=sl: y.dma_start(
                    out=Ksl[:, sl * SEQ:(sl + 1) * SEQ],
                    in_=s_kfm[pr, :, :]), waits=wk, inc=16, sem=sname)
                Y(lambda y, pr=pr, sl=sl, base=base: y.dma_start(
                    out=Qsl[:, sl * QB:(sl + 1) * QB],
                    in_=s_qfm[pr, :, base:base + QB]), waits=wk,
                  inc=16, sem=sname)
                tk = Y(lambda y, pr=pr, sl=sl: y.dma_start(
                    out=vsl[:, sl * 2144:sl * 2144 + 2080].rearrange(
                        "p (k x) -> p k x", k=16),
                    in_=s_vaug[:, :, 130 * pr:130 * (pr + 1)].rearrange(
                        "k p x -> p k x")), waits=wv_, inc=16, sem=sname)
                return tk

            tk_ld = [None] * 8
            if cy.get("preld") is not None:
                tk_ld[0], tk_ld[1] = cy["preld"]
                cy["preld"] = None
            else:
                tk_ld[0] = emit_loads(0)
                tk_ld[1] = emit_loads(1)

            def emit_pv(pr, c):
                sl = pr % 2
                po = (c % 4) * 1024
                tk = None
                vof = sl * 2144 + c * 130
                T(lambda t, c=c, po=po, vof=vof: t.matmul(
                    p_attA[:, :], vsl[:, vof:vof + 128],
                    PT[:, po:po + 512],
                    start=(c == 0), stop=(c == 15),
                    skip_group_check=True),
                    waits=[tk_exp[(pr, c)], tk_ld[pr],
                           (tk_evac[pr - 1] if pr >= 1 else cy["evac"])
                           if c == 0 else None], inc=0)
                tk = T(lambda t, c=c, po=po, vof=vof: t.matmul(
                    p_attB[:, :], vsl[:, vof + 65:vof + 193],
                    PT[:, po + 512:po + 1024],
                    start=(c == 0), stop=(c == 15),
                    skip_group_check=True), inc=1)
                tk_pvg[(pr, c)] = tk
                if c == 15:
                    tk_pv[pr] = tk

            for pr in range(8):
                if pr == 0 and pend_b is not None:
                    emit_bndA_y(pend_b)
                sl = pr % 2
                for c in range(16):
                    reg = (c % 2) * 1024
                    kof = sl * SEQ + c * 128
                    qof = sl * QB
                    if c >= 2:
                        wre = tk_exp.get((pr, c - 2))
                    elif pr >= 1:
                        wre = tk_exp[(pr - 1, 14 + c)]
                    else:
                        wre = cy["exp"].get(14 + c)
                    T(lambda t, kof=kof, qof=qof, reg=reg: t.matmul(
                        psc[:, reg:reg + 512],
                        Ksl[0:64, kof:kof + 128],
                        Qsl[0:64, qof:qof + QB],
                        start=True, stop=True, skip_group_check=True),
                        waits=[tk_ld[pr], wre], inc=0)
                    tk_qk = T(lambda t, kof=kof, qof=qof, reg=reg:
                              t.matmul(
                        psc[:, reg + 512:reg + 1024],
                        Ksl[64:128, kof:kof + 128],
                        Qsl[64:128, qof:qof + QB],
                        start=True, stop=True, skip_group_check=True),
                        inc=1)
                    if c == 15:
                        tk_qkl[pr] = tk_qk
                    if c >= 1:
                        emit_pv(pr, c - 1)
                    if c >= 4:
                        wslot = tk_pvg.get((pr, c - 4))
                    elif pr >= 1:
                        wslot = tk_pvg.get((pr - 1, 12 + c))
                    else:
                        wslot = cy["pvg"].get(12 + c)
                    po = (c % 4) * 1024
                    tk_exp[(pr, c)] = Sc(
                        lambda s, reg=reg, po=po: s.activation(
                            PT[:, po:po + 1024], psc[:, reg:reg + 1024],
                            AF.Exp, scale=0.125),
                        waits=[tk_qk, wslot], inc=1)
                emit_pv(pr, 15)
                V(lambda v: v.tensor_copy(rcb[0:1, 0:QB],
                                          p_attA[64:65, :]),
                  waits=[tk_pv[pr], cy.get("rcst")], inc=1)
                tk_rcb = V(lambda v: v.tensor_copy(rcb[0:1, QB:2 * QB],
                                                   p_attB[64:65, :]),
                           inc=1)
                cy["rcst"] = G(lambda g, pr=pr: g.dma_start(
                    out=s_rc[pr, :].unsqueeze(0), in_=rcb[0:1, :]),
                  waits=[tk_rcb, cy["dn"] if pr == 0 else None],
                  inc=16, sem="g_rc")
                V(lambda v, pr=pr, afm=attn_fm2[qb % 2]: v.tensor_copy(
                    fchunk(afm, pr)[0:64, :, :],
                    r3(p_attA[0:64, :])),
                  waits=[cy["afm_free"][qb % 2]], inc=1)
                tk_evac[pr] = V(lambda v, pr=pr, afm=attn_fm2[qb % 2]:
                                v.tensor_copy(
                    fchunk(afm, pr)[64:128, :, :],
                    r3(p_attB[0:64, :])), inc=1)
                if pr + 2 < 8:
                    tk_ld[pr + 2] = emit_loads(pr + 2)
                if pr == 2:
                    bnd_cur["x2t"][0] = emit_x2q(qb, 0, cy["x2c"][0])
                    bnd_cur["x2t"][1] = emit_x2q(qb, 1, cy["x2c"][1])
                if pend_b is not None:
                    if pr == 0:
                        emit_bndA_v(pend_b)
                    elif pr == 1:
                        emit_bndB(pend_b)
                    elif pr == 2:
                        emit_bndC(pend_b)
                    elif pr == 3:
                        emit_bndD(pend_b)
                    elif pr == 4:
                        emit_ffn1R(pend_b)
                    elif pr == 5:
                        emit_ffn1K(pend_b)
                    elif pr == 6:
                        emit_ffn1V(pend_b)
                        emit_ffn2(pend_b)
                    elif pr == 7:
                        emit_ffn3(pend_b)
                if pr == 7 and qb + 1 < NQB:
                    nbase = (qb + 1) * QB
                    cy["qkl"] = [tk_qkl[6], tk_qkl[7]]
                    cy["pv"] = [tk_pv[6], tk_pv[7]]
                    cy["preld"] = [emit_loads(0, base=nbase),
                                   emit_loads(1, base=nbase)]

            cy["qkl"] = [tk_qkl[6], tk_qkl[7]]
            cy["pv"] = [tk_pv[6], tk_pv[7]]
            cy["evac"] = tk_evac[7]
            cy["exp"] = {14: tk_exp[(7, 14)], 15: tk_exp[(7, 15)]}
            cy["pvg"] = {12 + i: tk_pvg[(7, 12 + i)] for i in range(4)}
            bnd_cur["evac7"] = tk_evac[7]
            pend_b = bnd_cur

        emit_bndA_y(pend_b)
        emit_bndA_v(pend_b)
        emit_bndB(pend_b)
        emit_bndC(pend_b)
        emit_bndD(pend_b)
        emit_ffn1R(pend_b)
        emit_ffn1K(pend_b)
        emit_ffn1V(pend_b)
        if s_i + 1 < NSEQ:
            tk_w01 = load_wset(["zq", "zk"], p.last["t"])
        emit_ffn2(pend_b)
        emit_ffn3(pend_b)
        if s_i + 1 < NSEQ:
            tk_wset = G(lambda g: g.dma_start(
                out=r3(w_sb[:, 2 * NCH * D:3 * NCH * D], D),
                in_=w_in["wvT"].rearrange("(c p) d -> p c d", p=128)),
                waits=[tk_w01], inc=16, sem="g_w")
        pend_b = None
        p.barrier()

    for s in ["g_w", "g_x", "g_x2a", "g_x2b", "g_qkst", "g_vst0",
              "g_vst1", "g_out0", "g_out1"]:
        if p.cnt[s]:
            G(lambda g, s=s: g.wait_ge(sems[s], p.cnt[s]))

    for s in ALL_SEMS:
        sems[s] = es.enter_context(nc.semaphore("sem_" + s))
    with nc.Block() as block, \
            nc.allow_low_precision(
                reason="bf16 intermediates; validated vs tolerance"):

        def replay(name, engobj):
            for ws, fn, inc, sem in p.ops[name]:
                for se, val in ws:
                    engobj.wait_ge(sems[se], val)
                if fn is None:
                    continue
                ins = fn(engobj)
                if inc:
                    ins.then_inc(sems[sem if sem else name], inc)

        @block.tensor
        def _(t):
            replay("t", t)

        @block.vector
        def _(v):
            replay("v", v)

        @block.scalar
        def _(s):
            replay("s", s)

        @block.sync
        def _(y):
            replay("y", y)

        @block.gpsimd
        def _(g):
            replay("g", g)

    es.close()
    return nc


def kernel(**inputs):
    x = np.ascontiguousarray(inputs["L0_attn_x"], dtype=np.float32)
    f32 = np.float32

    def pp(v):
        return np.ascontiguousarray(
            np.asarray(v, f32).reshape(NCH, 128).T)

    base = {
        "zq": np.ascontiguousarray(np.asarray(inputs["L0_attn_z_q"], f32)),
        "zk": np.ascontiguousarray(np.asarray(inputs["L0_attn_z_k"], f32)),
        "wvT": np.ascontiguousarray(np.asarray(inputs["L0_attn_w_v"], f32).T),
        "wrT": np.ascontiguousarray(np.asarray(inputs["L0_ffn_W_r"], f32).T),
        "wkT": np.ascontiguousarray(np.asarray(inputs["L0_ffn_W_k"], f32).T),
        "wfT": np.ascontiguousarray(np.asarray(inputs["L0_ffn_W_v"], f32).T),
        "zqT": np.ascontiguousarray(np.asarray(inputs["L0_attn_z_q"], f32).T),
        "zkT": np.ascontiguousarray(np.asarray(inputs["L0_attn_z_k"], f32).T),
        "bq": pp(inputs["L0_attn_b_q"]),
        "bk": pp(inputs["L0_attn_b_k"]),
        "mur": pp(inputs["L0_ffn_mu_r"]),
        "muk": pp(inputs["L0_ffn_mu_k"]),
        "bv": np.ascontiguousarray(np.broadcast_to(
            np.asarray(inputs["L0_attn_b_v"], f32), (128, D))),
        "ones": np.ones((128, 128), f32),
    }
    nc = build_nc()
    in_maps = []
    for core in range(8):
        m = dict(base)
        m["x"] = np.ascontiguousarray(x[2 * core:2 * core + 2])
        in_maps.append(m)
    res = run_bass_kernel_spmd(nc, in_maps, core_ids=list(range(8)))
    out = np.concatenate([r["out"] for r in res.results], axis=0)
    return np.ascontiguousarray(out.reshape(16, SEQ, D).astype(np.float32))
